# version 21
# baseline (speedup 1.0000x reference)
import contextlib
import numpy as np
import ml_dtypes
import concourse.bass as bass
import concourse.mybir as mybir
from concourse.bass_utils import run_bass_kernel_spmd

F32 = mybir.dt.float32
BF16 = mybir.dt.bfloat16
I32 = mybir.dt.int32
I16 = mybir.dt.int16
Alu = mybir.AluOpType
Act = mybir.ActivationFunctionType
AX = mybir.AxisListType.X

B, S, D = 4, 2048, 2048
H, KVH, DH = 16, 4, 128
NCORES = 8
NEG = -1e30
TQ = 1024

QBLOCKS = [[0, 3, 4, 7, 8, 11, 12, 15], [1, 2, 5, 6, 9, 10, 13, 14]]
MPAT = [1, 3, 5, 7, 9, 11, 13, 15]

ENGS = ("sync", "tensor", "vector", "scalar", "gpsimd")


NSLOT = 8


class Build:

    def __init__(self):
        self.entries = {e: [] for e in ENGS}
        self.cnt = {}
        self.slot_i = {"sync": 0, "gpsimd": 0, "scalar": 0}

    def emit(self, eng, f, wait=(), inc=()):
        self.entries[eng].append((tuple(wait), f, tuple(inc)))
        out = None
        for s, n in inc:
            self.cnt[s] = self.cnt.get(s, 0) + n
            out = self.cnt[s]
        return out

    QPFX = {"sync": "ld", "gpsimd": "gd", "scalar": "sd"}

    def dma(self, eng, f, wait=()):
        q = self.QPFX[eng]
        i = self.slot_i[eng] % NSLOT
        self.slot_i[eng] += 1
        s = f"{q}{i}"
        uses = self.cnt.get(s, 0)
        w = list(wait)
        if uses:
            w.append((s, uses))
        v = self.emit(eng, f, wait=w, inc=[(s, 16)])
        return (s, v)

    def dma_totals(self, eng):
        q = self.QPFX[eng]
        return [(f"{q}{i}", self.cnt.get(f"{q}{i}", 0)) for i in range(NSLOT)
                if self.cnt.get(f"{q}{i}", 0)]

    def cur(self, s):
        return self.cnt.get(s, 0)


def build_nc():
    nc = bass.Bass("TRN2", target_bir_lowering=False, debug=False)
    KJ, NB = 1024, 64

    x_full = nc.dram_tensor("x_full", [S, D], F32, kind="ExternalInput").ap()
    x_q = nc.dram_tensor("x_q", [TQ, D], F32, kind="ExternalInput").ap()
    wq_h = nc.dram_tensor("wq", [D, D], F32, kind="ExternalInput").ap()
    wk_h = nc.dram_tensor("wk", [D, KVH * DH], F32, kind="ExternalInput").ap()
    wv_h = nc.dram_tensor("wv", [D, KVH * DH], F32, kind="ExternalInput").ap()
    wo_h = nc.dram_tensor("wo", [D, D], F32, kind="ExternalInput").ap()
    gp_h = nc.dram_tensor("gp", [2048, KJ], I32, kind="ExternalInput").ap()
    gs_h = nc.dram_tensor("gs", [2048, NB], F32, kind="ExternalInput").ap()
    up_h = nc.dram_tensor("up", [2048, KJ], I32, kind="ExternalInput").ap()
    us_h = nc.dram_tensor("us", [2048, NB], F32, kind="ExternalInput").ap()
    dp_h = nc.dram_tensor("dp", [2048, KJ], I32, kind="ExternalInput").ap()
    ds_h = nc.dram_tensor("ds", [2048, NB], F32, kind="ExternalInput").ap()
    qg_h = nc.dram_tensor("qg", [128, 1], F32, kind="ExternalInput").ap()
    kg_h = nc.dram_tensor("kg", [128, 1], F32, kind="ExternalInput").ap()
    id_h = nc.dram_tensor("ident", [128, 128], BF16, kind="ExternalInput").ap()
    idf_h = nc.dram_tensor("identf", [128, 128], F32, kind="ExternalInput").ap()
    mk_h = nc.dram_tensor("maskqb", [8, 128, 512], F32, kind="ExternalInput").ap()
    y_h = nc.dram_tensor("y", [TQ, D], F32, kind="ExternalOutput").ap()

    wqb_h = nc.dram_tensor("wq_bf", [D, D], BF16).ap()
    wkb_h = nc.dram_tensor("wk_bf", [D, KVH * DH], BF16).ap()
    wvb_h = nc.dram_tensor("wv_bf", [D, KVH * DH], BF16).ap()
    wob_h = nc.dram_tensor("wo_bf", [D, D], BF16).ap()
    g_sh = nc.dram_tensor("g_sh", [16, 128, 16, 128], BF16)
    u_sh = nc.dram_tensor("u_sh", [16, 128, 16, 128], BF16)
    d_sh = nc.dram_tensor("d_sh", [16, 128, 16, 128], BF16)
    g_fu = nc.dram_tensor("g_fu", [128, 128, 16, 128], BF16)
    u_fu = nc.dram_tensor("u_fu", [128, 128, 16, 128], BF16)
    d_fu = nc.dram_tensor("d_fu", [128, 128, 16, 128], BF16)

    bld = Build()
    E, C = bld.emit, bld.cur

    def vbar():
        return E("vector", lambda e: e.drain(), inc=[("v", 1)])

    def sbar():
        return E("scalar", lambda e: e.drain(), inc=[("s", 1)])

    with contextlib.ExitStack() as ST:
        alloc = ST.enter_context

        ar1 = alloc(nc.sbuf_tensor("ar1", [128, 32768], BF16))
        ar2 = alloc(nc.sbuf_tensor("ar2", [128, 16384], BF16))
        ar3 = alloc(nc.sbuf_tensor("ar3", [128, 16384], BF16))
        ar4 = alloc(nc.sbuf_tensor("ar4", [128, 8192], BF16))
        ar5 = alloc(nc.sbuf_tensor("ar5", [128, 8192], BF16))
        ar6 = alloc(nc.sbuf_tensor("ar6", [128, 8192], BF16))
        ar7 = alloc(nc.sbuf_tensor("ar7", [128, 8192], BF16))
        arp = alloc(nc.sbuf_tensor("arp", [128, 4096], BF16))
        id_sb = alloc(nc.sbuf_tensor("id_sb", [128, 128], BF16))
        idf_sb = alloc(nc.sbuf_tensor("idf_sb", [128, 128], F32))
        qg_sb = alloc(nc.sbuf_tensor("qg_sb", [128, 1], F32))
        kg_sb = alloc(nc.sbuf_tensor("kg_sb", [128, 1], F32))
        ones_b = alloc(nc.sbuf_tensor("ones_b", [128, 1], BF16))
        ones_f = alloc(nc.sbuf_tensor("ones_f", [1, 128], F32))
        nmax = alloc(nc.sbuf_tensor("nmax", [128, 1], F32))
        nmax2 = alloc(nc.sbuf_tensor("nmax2", [128, 1], F32))
        rsum = alloc(nc.sbuf_tensor("rsum", [128, 1], F32))
        rcp = alloc(nc.sbuf_tensor("rcp", [128, 1], F32))
        nmean = alloc(nc.sbuf_tensor("nmean", [128, 1], F32))
        ssd = alloc(nc.sbuf_tensor("ssd", [128, 1], F32))
        rstd = alloc(nc.sbuf_tensor("rstd", [128, 1], F32))
        sq2 = alloc(nc.sbuf_tensor("sq2", [128, 2048], BF16))
        eps8 = alloc(nc.sbuf_tensor("eps8", [128, 1], F32))
        eps5 = alloc(nc.sbuf_tensor("eps5", [128, 1], F32))
        ps = alloc(nc.psum_tensor("ps", [128, 4096], F32))

        SEM = {}
        semnames = ["t", "v", "s", "cc", "wc"]
        semnames += [f"ld{i}" for i in range(NSLOT)] + [f"gd{i}" for i in range(NSLOT)]
        semnames += [f"sd{i}" for i in range(NSLOT)]
        for s_ in semnames:
            SEM[s_] = alloc(nc.semaphore(s_))
        blk = alloc(nc.Block())

        def bank(i, n=1):
            return ps[:, 512 * i:512 * (i + n)]

        def bank_bf(i, n=1):
            return ps[:, 512 * i:512 * (i + n)].bitcast(BF16)

        xfT = ar1[:].rearrange("p (a b) -> p a b", b=2048)
        wo_sb = ar1[:].rearrange("p (a b) -> p a b", b=2048)
        y2acc = ar1[:].bitcast(F32).rearrange("p (a b) -> p a b", b=1024)
        xqT = ar2[:].rearrange("p (a b) -> p a b", b=1024)
        wv_sb = ar2[:, 8192:16384].rearrange("p (a b) -> p a b", b=512)
        ao_sb = ar2[:].rearrange("p (a b) -> p a b", b=1024)
        inter = ar2[:].rearrange("p (a b) -> p a b", b=1024)
        qhat = ar3[:].rearrange("p (a b) -> p a b", b=1024)
        y1T = ar3[:].rearrange("p (a b) -> p a b", b=1024)
        khat = ar4[:].rearrange("p (a b) -> p a b", b=2048)
        gub = ar4[:].rearrange("p (a b c) -> p a b c", b=16, c=128)
        v_sb = ar5[:].rearrange("p (a b) -> p a b", b=512)
        dnb = ar5[:, 0:4096].rearrange("p (a b c) -> p a b c", b=16, c=128)
        y1pre = ar5[:, 0:4096].bitcast(F32)
        dq_b32 = [ar3[:, i * 2048:(i + 1) * 2048].bitcast(I32) for i in range(6)]
        dq_b16 = [ar3[:, 12288:13312].bitcast(I16), ar3[:, 13312:14336].bitcast(I16)]
        dq_m = arp[:, 0:1024].bitcast(I16)
        dq_m2 = arp[:, 1024:2048].bitcast(I16)
        dq_bb = arp[:, 2048:3072].bitcast(I16)
        dq_bbh = arp[:, 3072:4096].bitcast(I16)
        dq_s2 = sq2[:, 0:1024].bitcast(I16)
        dq_s2h = sq2[:, 1024:2048].bitcast(I16)
        dq_wraw = [ar5[:, 5120:7168], ar4[:, 6144:8192]]
        dq_bits = [w.bitcast(I16).rearrange("p (a b) -> p a b", b=1024) for w in dq_wraw]
        dq_w = [w.rearrange("p (a b) -> p a b", b=1024) for w in dq_wraw]
        dq_sc = [ar5[:, i * 128:(i + 1) * 128].bitcast(F32) for i in range(6)]
        dq_st = [ar5[:, 1024:3072].rearrange("p (a b) -> p a b", b=128),
                 ar5[:, 3072:5120].rearrange("p (a b) -> p a b", b=128)]
        masks = ar7[:].bitcast(F32).rearrange("p (a b) -> p a b", b=512)
        pcs_f = [ar4[:, i * 2048:(i + 1) * 2048].bitcast(F32) for i in range(3)]
        pcs_b = [ar3[:, 14336:15360], ar3[:, 15360:16384], ar5[:, 7168:8192]]
        xf32 = [ar6[:, i * 2048:(i + 1) * 2048].bitcast(F32) for i in range(4)]
        xres = [ar6[:, 0:4096].bitcast(F32), ar6[:, 4096:8192].bitcast(F32)]
        y2tok = [ar6[:, 0:4096].bitcast(F32), ar6[:, 4096:8192].bitcast(F32)]
        ln2a = arp[:, 0:1024].bitcast(F32)[0:1, 0:512]
        ln2b = arp[:, 1024:2048].bitcast(F32)[0:1, 0:512]
        ln2c = arp[:, 2048:3072].bitcast(F32)[0:1, 0:512]
        P_sb = arp[:, 0:2048]
        PT_sb = arp[:, 2048:4096].rearrange("p (a b) -> p a b", b=128)

        DMA = bld.dma
        id_ms = DMA("sync", lambda e: e.dma_start(out=id_sb[:], in_=id_h))
        idf_ms = DMA("sync", lambda e: e.dma_start(out=idf_sb[:], in_=idf_h))
        qg_ms = DMA("sync", lambda e: e.dma_start(out=qg_sb[:], in_=qg_h))
        kg_ms = DMA("sync", lambda e: e.dma_start(out=kg_sb[:], in_=kg_h))
        for qb in range(8):
            DMA("sync", lambda e, qb=qb: e.dma_start(out=masks[:, qb, :], in_=mk_h[qb]))
        const_ms = bld.dma_totals("sync")
        E("vector", lambda e: e.memset(ones_b[:], 1.0))
        E("vector", lambda e: e.memset(ones_f[:], 1.0))
        E("vector", lambda e: e.memset(eps8[:], 1e-8))
        E("vector", lambda e: e.memset(eps5[:], 1e-5))
        E("vector", lambda e: e.drain())

        pc_items = []
        for kc in range(16):
            for hf in range(2):
                pc_items.append((wq_h[kc * 128:kc * 128 + 128,
                                      hf * 1024:hf * 1024 + 1024],
                                 wqb_h[kc * 128:kc * 128 + 128,
                                       hf * 1024:hf * 1024 + 1024], 1024))
        for kc in range(16):
            pc_items.append((wk_h[kc * 128:kc * 128 + 128, :],
                             wkb_h[kc * 128:kc * 128 + 128, :], 512))
            pc_items.append((wv_h[kc * 128:kc * 128 + 128, :],
                             wvb_h[kc * 128:kc * 128 + 128, :], 512))
        for kc in range(16):
            for hf in range(2):
                pc_items.append((wo_h[kc * 128:kc * 128 + 128,
                                      hf * 1024:hf * 1024 + 1024],
                                 wob_h[kc * 128:kc * 128 + 128,
                                       hf * 1024:hf * 1024 + 1024], 1024))
        pc_lm = {}
        pc_sm = [0, 0, 0]

        def pc_load(it):
            if it >= len(pc_items):
                return
            src_, dst_, w_ = pc_items[it]
            sl = it % 3
            wld = [pc_sm[sl]] if pc_sm[sl] else []
            pc_lm[it] = DMA("gpsimd", lambda e, src_=src_, sl=sl, w_=w_:
                            e.dma_start(out=pcs_f[sl][:, 0:w_], in_=src_),
                            wait=wld)

        def pc_proc(i0):
            for it in (i0, i0 + 1):
                sl = it % 3
                w_ = pc_items[it][2]
                E("scalar", lambda e, sl=sl, w_=w_: e.activation(
                    pcs_b[sl][:, 0:w_], pcs_f[sl][:, 0:w_], Act.Identity),
                  wait=[pc_lm[it]])
            pcst = sbar()
            for it in (i0, i0 + 1):
                sl = it % 3
                dst_, w_ = pc_items[it][1], pc_items[it][2]
                pc_sm[sl] = DMA("sync", lambda e, dst_=dst_, sl=sl, w_=w_:
                                e.dma_start(out=dst_, in_=pcs_b[sl][:, 0:w_]),
                                wait=[("s", pcst)])

        def dq_chain(v, b6, t2):
            b16v = dq_b16[t2]
            v.tensor_copy(b16v[:], dq_b32[b6][:])
            v.drain()
            v.tensor_single_scalar(dq_m[:], b16v[:], 7, Alu.bitwise_and)
            v.tensor_single_scalar(dq_m2[:], b16v[:], 0x70, Alu.bitwise_and)
            v.tensor_single_scalar(dq_s2[:], b16v[:], 8, Alu.bitwise_and)
            v.tensor_single_scalar(dq_s2h[:], b16v[:], 128, Alu.bitwise_and)
            v.tensor_scalar(dq_bb[:], dq_m[:], 2, 16128, op0=Alu.min, op1=Alu.mult)
            v.tensor_scalar(dq_bbh[:], dq_m2[:], 32, 1008, op0=Alu.min, op1=Alu.mult)
            v.tensor_single_scalar(dq_s2[:], dq_s2[:], -4096, Alu.mult)
            v.tensor_single_scalar(dq_s2h[:], dq_s2h[:], -256, Alu.mult)
            v.tensor_single_scalar(dq_m[:], dq_m[:], 64, Alu.mult)
            v.tensor_single_scalar(dq_m2[:], dq_m2[:], 4, Alu.mult)
            v.scalar_tensor_tensor(dq_m[:], dq_m[:], 16128.0, dq_bb[:],
                                   op0=Alu.add, op1=Alu.min)
            v.scalar_tensor_tensor(dq_m2[:], dq_m2[:], 16128.0, dq_bbh[:],
                                   op0=Alu.add, op1=Alu.min)
            v.tensor_tensor(dq_bits[t2][:, 0, :], dq_m[:], dq_s2[:], Alu.add)
            v.tensor_tensor(dq_bits[t2][:, 1, :], dq_m2[:], dq_s2h[:], Alu.add)
            sc_b = dq_sc[b6][:].unsqueeze(2).broadcast_to([128, 64, 16])
            for pl in range(2):
                bits_bf = dq_bits[t2][:, pl, :].bitcast(BF16).rearrange(
                    "n (b u) -> n b u", u=16)
                v.tensor_tensor(dq_w[t2][:, pl, :].rearrange(
                    "n (b u) -> n b u", u=16), bits_bf, sc_b, Alu.mult)
            return v.drain()

        mats = ((gp_h, gs_h, g_sh), (up_h, us_h, u_sh), (dp_h, ds_h, d_sh))
        a1_items = ([(x_full, xfT, tt, hf) for tt in range(16) for hf in range(2)]
                    + [(x_q, xqT, tt, hf) for tt in range(8) for hf in range(2)])
        a1_lm = {}
        a1_tr = [0, 0, 0, 0]
        a1_ev = [0, 0]
        a1_g = 0

        def a1_load(hi):
            if hi >= len(a1_items):
                return
            xd, dst, tt, hf = a1_items[hi]
            sl = hi % 4
            wt0 = [("t", a1_tr[sl])] if a1_tr[sl] else []
            a1_lm[hi] = DMA("sync", lambda e, xd=xd, tt=tt, hf=hf, sl=sl:
                            e.dma_start(out=xf32[sl][:],
                                        in_=xd[tt * 128:tt * 128 + 128,
                                               hf * 1024:hf * 1024 + 1024]),
                            wait=wt0)

        dq_cons = [0] * 6

        def dq_load(g2):
            if g2 >= 48:
                return (None, None)
            mi2, oc2 = g2 // 16, g2 % 16
            pk2, scl2 = mats[mi2][0], mats[mi2][1]
            b62 = g2 % 6
            wld = []
            mm1 = DMA("scalar", lambda e, pk2=pk2, oc2=oc2, b62=b62: e.dma_start(
                out=dq_b32[b62][:],
                in_=pk2[oc2 * 128:oc2 * 128 + 128, :].rearrange(
                    "(j two) c -> two j c", two=2)), wait=wld)
            mm2 = DMA("scalar", lambda e, scl2=scl2, oc2=oc2, b62=b62: e.dma_start(
                out=dq_sc[b62][:],
                in_=scl2[oc2 * 128:oc2 * 128 + 128, :].rearrange(
                    "(j two) c -> two j c", two=2)))
            return (mm1, mm2)

        dq_lms = {}
        dq_tr = [0, 0]
        dq_evac = [0, 0]
        dq_store = [0, 0]
        dq_vm = 0
        a1_load(0)
        a1_load(1)
        a1_load(2)
        pc_load(0)
        pc_load(1)
        for g2 in range(6):
            dq_lms[g2] = dq_load(g2)
        for goc in range(48):
            mi, oc = goc // 16, goc % 16
            sh = mats[mi][2]
            t2 = goc % 2
            m1, m2 = dq_lms[goc]
            wv_ = [m1, m2]
            if dq_tr[t2]:
                wv_.append(("t", dq_tr[t2]))
            dq_vm = E("vector", lambda e, b6=goc % 6, t2=t2: dq_chain(e, b6, t2),
                      wait=wv_, inc=[("v", 1)])
            dq_cons[goc % 6] = dq_vm
            for bh in range(2):
                wt = [("v", dq_vm)]
                if dq_evac[bh]:
                    wt.append(("s", dq_evac[bh]))
                for kcl in range(8):
                    kc = bh * 8 + kcl
                    E("tensor", lambda e, bh=bh, kcl=kcl, kc=kc, t2=t2:
                      e.transpose(
                          bank_bf(6 + bh)[0:64, kcl * 128:kcl * 128 + 128],
                          dq_w[t2][:, 0, kc * 64:kc * 64 + 64], id_sb[:]),
                      wait=wt if kcl == 0 else ())
                    E("tensor", lambda e, bh=bh, kcl=kcl, kc=kc, t2=t2:
                      e.transpose(
                          bank_bf(6 + bh)[64:128, kcl * 128:kcl * 128 + 128],
                          dq_w[t2][:, 1, kc * 64:kc * 64 + 64], id_sb[:]),
                      inc=[("t", 1)] if kcl == 7 else ())
                tm = C("t")
                wev = [("t", tm)]
                if dq_store[t2]:
                    wev.append(dq_store[t2])
                dq_evac[bh] = E("scalar", lambda e, bh=bh, t2=t2: e.copy(
                    dq_st[t2][:, bh * 8:bh * 8 + 8, :],
                    bank_bf(6 + bh)[:].rearrange("p (a b) -> p a b", b=128)),
                    wait=wev, inc=[("s", 1)])
            dq_tr[t2] = C("t")
            dq_lms[goc + 6] = dq_load(goc + 6)
            dq_store[t2] = DMA("scalar", lambda e, sh=sh, oc=oc, t2=t2:
                               e.dma_start(out=sh.ap()[oc], in_=dq_st[t2][:]),
                               wait=[("s", dq_evac[1])])
            xd, dst, tt, hf = a1_items[goc]
            sl = goc % 4
            for sub in range(2):
                bk = 4 + (a1_g % 2)
                wt2 = [a1_lm[goc]] if sub == 0 else []
                if a1_ev[a1_g % 2]:
                    wt2.append(("s", a1_ev[a1_g % 2]))
                for q in range(4):
                    kcl = sub * 4 + q
                    E("tensor", lambda e, sl=sl, kcl=kcl, bk=bk, q=q:
                      e.transpose(
                          ps[:, bk * 512 + q * 128:bk * 512 + q * 128 + 128],
                          xf32[sl][:, kcl * 128:kcl * 128 + 128], idf_sb[:]),
                      wait=wt2 if q == 0 else (),
                      inc=[("t", 1)] if q == 3 else ())
                tm = C("t")
                if sub == 1:
                    a1_tr[sl] = tm
                a1_ev[a1_g % 2] = E(
                    "scalar",
                    lambda e, dst=dst, tt=tt, hf=hf, sub=sub, bk=bk: e.copy(
                        dst[:, hf * 8 + sub * 4:hf * 8 + sub * 4 + 4,
                            tt * 128:tt * 128 + 128],
                        ps[:, bk * 512:bk * 512 + 512].rearrange(
                            "p (a b) -> p a b", b=128)),
                    wait=[("t", tm)], inc=[("s", 1)])
                a1_g += 1
            a1_load(goc + 2)
            pc_proc(2 * goc)
            pc_load(2 * goc + 2)
            pc_load(2 * goc + 3)
        xT_done_v = ("s", C("s"))
        dq_vm_last = dq_vm
        t_ph0 = C("t")
        pc_fin = bld.dma_totals("sync") + bld.dma_totals("scalar")
        dq_store_last = [dq_store[0], dq_store[1]]
        ag_w = (bld.dma_totals("scalar") + bld.dma_totals("sync")
                + bld.dma_totals("gpsimd"))

        rms_ln = arp[:].bitcast(F32)[0:1, 0:2048]

        def rms_apply(raw_ap, out_aps, nblk, reg, gam, extra=(), bank_guard=0):
            sqv = sq2[:, 0:nblk * 512]
            E("scalar", lambda e: e.activation(sqv, raw_ap, Act.Square), wait=extra)
            sm = sbar()
            for b_ in range(nblk):
                w_ss = [("s", sm)] + ([("v", bank_guard)] if bank_guard else [])
                E("tensor", lambda e, b_=b_: e.matmul(
                    ps[0:1, (reg + b_) * 512:(reg + b_) * 512 + 512], ones_b[:],
                    sqv[:, b_ * 512:b_ * 512 + 512], start=True, stop=True),
                  wait=w_ss if b_ == 0 else (),
                  inc=[("t", 1)] if b_ == nblk - 1 else ())
            tm = C("t")
            E("scalar", lambda e: e.activation(
                rms_ln[:, 0:nblk * 512], ps[0:1, reg * 512:(reg + nblk) * 512],
                Act.Ln, scale=1.0 / 128, bias=eps8[0:1, :]), wait=[("t", tm)])
            E("scalar", lambda e: e.drain())
            E("scalar", lambda e: e.activation(rms_ln[:, 0:nblk * 512],
                                               rms_ln[:, 0:nblk * 512], Act.Exp,
                                               scale=-0.5))
            sm2 = sbar()
            for b_ in range(nblk):
                E("tensor", lambda e, b_=b_: e.matmul(
                    ps[:, (reg + b_) * 512:(reg + b_) * 512 + 512], ones_f[:],
                    rms_ln[:, b_ * 512:b_ * 512 + 512], start=True, stop=True),
                  wait=[("s", sm2)] if b_ == 0 else (),
                  inc=[("t", 1)] if b_ == nblk - 1 else ())
            tm2 = C("t")
            for b_ in range(nblk):
                E("vector", lambda e, b_=b_, gam=gam: e.scalar_tensor_tensor(
                    out_aps[b_], ps[:, (reg + b_) * 512:(reg + b_) * 512 + 512],
                    gam[:], raw_ap[:, b_ * 512:b_ * 512 + 512],
                    op0=Alu.mult, op1=Alu.mult),
                  wait=[("t", tm2)] if b_ == 0 else ())
            return vbar()

        wslab = [ar4[:, i * 1024:(i + 1) * 1024] for i in range(8)]
        slab_mm = [0] * 8
        bank_ev = [0] * 8
        qev = 0
        si = 0
        for thh in range(4):
            hh8, th = thh // 2, thh % 2
            for kc in range(16):
                s8 = si % 8
                wld = [("t", slab_mm[s8])] if slab_mm[s8] else (
                    list(pc_fin) + [("t", t_ph0)] if si == 0 else [])
                lm = DMA("sync", lambda e, hh8=hh8, kc=kc, s8=s8: e.dma_start(
                    out=wslab[s8][:],
                    in_=wqb_h[kc * 128:kc * 128 + 128,
                              hh8 * 1024:hh8 * 1024 + 1024]), wait=wld)
                wmm = [lm]
                if thh == 0 and kc == 0:
                    wmm += [xT_done_v, ("v", dq_vm_last)]
                for h8 in range(8):
                    wmh = list(wmm) if h8 == 0 else []
                    if kc == 0 and bank_ev[h8]:
                        wmh.append(("s", bank_ev[h8]))
                    E("tensor", lambda e, h8=h8, kc=kc, th=th, s8=s8: e.matmul(
                        bank(h8), wslab[s8][:, h8 * 128:h8 * 128 + 128],
                        xqT[:, kc, th * 512:th * 512 + 512],
                        start=(kc == 0), stop=(kc == 15)),
                      wait=wmh,
                      inc=[("t", 1)] if h8 == 7 else ())
                slab_mm[s8] = C("t")
                si += 1
            tq_ = C("t")
            for h8 in range(8):
                wq_ev = [("t", tq_)] if h8 == 0 else []
                if h8 == 0 and thh == 0:
                    wq_ev.append(("v", dq_vm_last))
                if h8 == 0 and thh == 2:
                    wq_ev += pc_fin
                bank_ev[h8] = E("scalar", lambda e, hh8=hh8, th=th, h8=h8: e.copy(
                    qhat[:, hh8 * 8 + h8, th * 512:th * 512 + 512], bank(h8)),
                  wait=wq_ev, inc=[("s", 1)])
            qev = C("s")
        q_all_t = C("t")
        wk_sb = ar6[:].rearrange("p (a b) -> p a b", b=512)
        wk_lm = DMA("sync", lambda e: e.dma_start(
            out=wk_sb[:], in_=wkb_h.rearrange("(c p) n -> p c n", p=128)),
            wait=[("t", t_ph0)])
        for th in range(2):
            for kc in range(16):
                for kv in range(4):
                    for t2 in range(2):
                        wmh = [wk_lm] if (kc == 0 and kv == 0 and t2 == 0) else []
                        if kc == 0 and bank_ev[kv * 2 + t2]:
                            wmh.append(("s", bank_ev[kv * 2 + t2]))
                        E("tensor", lambda e, kv=kv, t2=t2, kc=kc, th=th:
                          e.matmul(bank(kv * 2 + t2),
                                   wk_sb[:, kc, kv * 128:kv * 128 + 128],
                                   xfT[:, kc, th * 1024 + t2 * 512:
                                       th * 1024 + t2 * 512 + 512],
                                   start=(kc == 0), stop=(kc == 15)),
                          wait=wmh,
                          inc=[("t", 1)] if (kv == 3 and t2 == 1) else ())
            tk_ = C("t")
            for kv in range(4):
                bank_ev[kv * 2] = bank_ev[kv * 2 + 1] = E(
                    "scalar", lambda e, kv=kv, th=th: e.copy(
                        khat[:, kv, th * 1024:th * 1024 + 1024], bank(kv * 2, 2)),
                    wait=[("t", tk_)] if kv == 0 else (),
                    inc=[("s", 1)])
            qev = C("s")
        wv_lm = DMA("sync", lambda e: e.dma_start(
            out=wv_sb[:], in_=wvb_h.rearrange("(c p) n -> p c n", p=128)),
            wait=[("t", q_all_t)])
        rm = 0
        prev_rm = [("s", qev)]
        reg_guard = {0: 0, 4: 0}
        for wi in range(12):
            reg = (wi % 2) * 4
            if wi < 8:
                outs_ = [qhat[:, wi * 2 + b_ // 2, (b_ % 2) * 512:(b_ % 2) * 512 + 512]
                         for b_ in range(4)]
                raw = ar3[:, (wi * 2) * 1024:(wi * 2 + 2) * 1024]
                rm = rms_apply(raw, outs_, 4, reg, qg_sb, extra=prev_rm,
                               bank_guard=reg_guard[reg])
            else:
                kv = wi - 8
                outs_ = [khat[:, kv, b_ * 512:b_ * 512 + 512] for b_ in range(4)]
                raw = ar4[:, kv * 2048:(kv + 1) * 2048]
                rm = rms_apply(raw, outs_, 4, reg, kg_sb, extra=prev_rm,
                               bank_guard=reg_guard[reg])
            reg_guard[reg] = rm
            prev_rm = []
        kv_done = rm
        v_ev = [0, 0]
        for tt in range(16):
            reg = tt % 2
            wmm = [wv_lm, ("v", kv_done)] if tt < 2 else []
            if v_ev[reg]:
                wmm.append(("s", v_ev[reg]))
            for kc in range(16):
                E("tensor", lambda e, tt=tt, kc=kc, reg=reg: e.matmul(
                    bank(reg), xfT[:, kc, tt * 128:tt * 128 + 128], wv_sb[:, kc, :],
                    start=(kc == 0), stop=(kc == 15)),
                  wait=wmm if kc == 0 else (),
                  inc=[("t", 1)] if kc == 15 else ())
            vt = C("t")
            wve = [("t", vt)]
            if tt == 0:
                wve += dq_store_last
            v_ev[reg] = E("scalar", lambda e, tt=tt, reg=reg: e.copy(
                v_sb[:, tt, :], bank(reg)), wait=wve, inc=[("s", 1)])
        v_done = ("s", C("s"))
        ag_w = ag_w + [v_done]
        for sh_, fu_ in ((g_sh, g_fu), (u_sh, u_fu), (d_sh, d_fu)):
            E("gpsimd", lambda e, sh_=sh_, fu_=fu_: e.collective_compute(
                "AllGather", Alu.bypass, replica_groups=[list(range(NCORES))],
                ins=[sh_.ap().opt()], outs=[fu_.ap().opt()]),
              wait=ag_w, inc=[("cc", 1)])
            ag_w = ()
        ag_done = C("cc")


        for kc in range(16):
            DMA("sync", lambda e, kc=kc: e.dma_start(
                out=wo_sb[:, kc, :], in_=wob_h[kc * 128:kc * 128 + 128, :]),
                wait=[v_done] if kc == 0 else [])
        wo_lms = bld.dma_totals("sync")

        groups = [(qb, h) for qb in range(8) for h in range(H)]
        ng = len(groups)

        def geo(qb):
            M = MPAT[qb]
            nkt = M + 1
            ngr = (nkt + 3) // 4
            return nkt, ngr, ngr * 512

        def region(qb, h):
            ngr = geo(qb)[1]
            if ngr <= 2:
                return (h % 2) * 2
            if ngr == 3:
                return (h % 2) * 3
            return 0

        vsc_prev = 0
        bank_guard = [0, 0, 0, 0, 0, 0]
        pt_t_by_par = [0, 0]
        pt_ev = 0
        pv_ev = 0
        pvap = ps[:, 3584:3712]
        Pbuf = [P_sb, sq2]

        def emit_S(i):
            qb, h = groups[i]
            kv = h // 4
            nkt, ngr, L5 = geo(qb)
            r0 = region(qb, h)
            w = []
            gmax = max(bank_guard[r0:r0 + ngr])
            if gmax:
                w.append(("s", gmax))
            if i == 0:
                w = [v_done] + const_ms + w
            for g in range(ngr):
                E("tensor", lambda e, qb=qb, h=h, kv=kv, g=g, r0=r0: e.matmul(
                    bank(r0 + g), qhat[:, h, qb * 128:qb * 128 + 128],
                    khat[:, kv, g * 512:g * 512 + 512], start=True, stop=True),
                  wait=w if g == 0 else (),
                  inc=[("t", 1)] if g == ngr - 1 else ())
            return C("t")

        nmaxs = [nmax, nmax2]

        def emit_stats(i, tS):
            qb = groups[i][0]
            nkt, ngr, L5 = geo(qb)
            r0 = region(qb, groups[i][1])
            Sap = ps[:, r0 * 512:r0 * 512 + L5]
            nm_ = nmaxs[i % 2]
            E("vector", lambda e, qb=qb, Sap=Sap, L5=L5: e.tensor_tensor(
                Sap[:, L5 - 512:L5], Sap[:, L5 - 512:L5], masks[:, qb, :], Alu.add),
              wait=[("t", tS)])
            E("vector", lambda e: e.drain())
            E("vector", lambda e, Sap=Sap, nm_=nm_: e.tensor_reduce(
                nm_[:], Sap, axis=AX, op=Alu.max, negate=True))
            E("vector", lambda e: e.drain())
            E("vector", lambda e, nm_=nm_: e.tensor_single_scalar(
                nm_[:], nm_[:], 0.08838834764831845, Alu.mult))
            return vbar()

        tS_ms = emit_S(0)
        vmx = emit_stats(0, tS_ms)
        for i in range(ng):
            qb, h = groups[i]
            kv = h // 4
            nkt, ngr, L5 = geo(qb)
            r0 = region(qb, h)
            Sap = ps[:, r0 * 512:r0 * 512 + L5]
            Pv = Pbuf[i % 2]
            wexp = [("v", vmx)]
            if pt_t_by_par[i % 2]:
                wexp.append(("t", pt_t_by_par[i % 2]))
            E("scalar", lambda e, Sap=Sap, L5=L5, Pv=Pv, i=i: e.activation(
                Pv[:, 0:L5], Sap, Act.Exp, bias=nmaxs[i % 2][:],
                scale=0.08838834764831845, accum_out=rsum[:]), wait=wexp)
            em = sbar()
            for b_ in range(r0, r0 + ngr):
                bank_guard[b_] = em
            if i + 1 < ng:
                tS_ms = emit_S(i + 1)
                vmx = emit_stats(i + 1, tS_ms)
            wln_ = [("v", vsc_prev)] if vsc_prev else []
            E("scalar", lambda e: e.activation(rcp[:], rsum[:], Act.Ln), wait=wln_)
            E("scalar", lambda e: e.drain())
            E("scalar", lambda e: e.activation(rcp[:], rcp[:], Act.Exp, scale=-1.0))
            smn = sbar()
            E("vector", lambda e, L5=L5, Pv=Pv: e.tensor_scalar(
                Pv[:, 0:L5], Pv[:, 0:L5], rcp[:], None, op0=Alu.mult),
              wait=[("s", smn)])
            vsc_prev = vbar()
            vsc = ("v", vsc_prev)
            wPT = [vsc]
            nw = (nkt + 7) // 8
            for w_ in range(nw):
                t0 = w_ * 8
                t1 = min(nkt, t0 + 8)
                for kt in range(t0, t1):
                    E("tensor", lambda e, kt=kt, t0=t0, Pv=Pv: e.transpose(
                        bank_bf(6)[:, (kt - t0) * 128:(kt - t0) * 128 + 128],
                        Pv[:, kt * 128:kt * 128 + 128], id_sb[:]),
                      wait=wPT if kt == t0 else (),
                      inc=[("t", 1)] if kt == t1 - 1 else ())
                tw = C("t")
                if w_ == nw - 1:
                    pt_t_by_par[i % 2] = tw
                pt_ev = E("scalar", lambda e, t0=t0, t1=t1: e.copy(
                    PT_sb[:, t0:t1, :],
                    bank_bf(6)[:, 0:(t1 - t0) * 128].rearrange(
                        "p (a b) -> p a b", b=128)),
                    wait=[("t", tw)], inc=[("s", 1)])
                wPT = [("s", pt_ev)]
            wPV = [("s", pt_ev)]
            if pv_ev:
                wPV.append(("s", pv_ev))
            for kt in range(nkt):
                E("tensor", lambda e, kt=kt, kv=kv: e.matmul(
                    pvap, v_sb[:, kt, kv * 128:kv * 128 + 128], PT_sb[:, kt, :],
                    start=(kt == 0), stop=(kt == nkt - 1)),
                  wait=wPV if kt == 0 else (),
                  inc=[("t", 1)] if kt == nkt - 1 else ())
            tpv = C("t")
            pv_ev = E("scalar", lambda e, h=h, qb=qb: e.copy(
                ao_sb[:, h, qb * 128:qb * 128 + 128], pvap),
                wait=[("t", tpv)], inc=[("s", 1)])
        a3_done_s = C("s")

        vst_prev = 0
        sst_prev = 0
        y1t_last_t = 0
        y1t_ev = [0, 0]
        xres_used = [0, 0]
        pend = None

        def y1t_emit(ptc, lnv):
            nonlocal_state = None
            for bh in range(2):
                wt2 = [("v", lnv)]
                if y1t_ev[bh]:
                    wt2.append(("v", y1t_ev[bh]))
                for kcl in range(8):
                    kc = bh * 8 + kcl
                    E("tensor", lambda e, bh=bh, kcl=kcl, kc=kc: e.transpose(
                        bank_bf(4 + bh)[0:64, kcl * 128:kcl * 128 + 128],
                        P_sb[:, kc * 128:kc * 128 + 128:2], id_sb[:]),
                      wait=wt2 if kcl == 0 else ())
                    E("tensor", lambda e, bh=bh, kcl=kcl, kc=kc: e.transpose(
                        bank_bf(4 + bh)[64:128, kcl * 128:kcl * 128 + 128],
                        P_sb[:, kc * 128 + 1:kc * 128 + 128:2], id_sb[:]),
                      inc=[("t", 1)] if kcl == 7 else ())
                tm2 = C("t")
                y1t_ev[bh] = E("vector", lambda e, bh=bh, ptc=ptc: e.tensor_copy(
                    y1T[:, bh * 8:bh * 8 + 8, ptc * 128:ptc * 128 + 128],
                    bank_bf(4 + bh)[:].rearrange("p (a b) -> p a b", b=128)),
                    wait=[("t", tm2)], inc=[("v", 1)])
            return C("t")

        for tc in range(8):
            bu = tc % 2
            wxr = [("s", a3_done_s)] if tc < 2 else [("v", xres_used[bu])]
            xm = DMA("sync", lambda e, tc=tc, bu=bu: e.dma_start(
                out=xres[bu][:], in_=x_q[tc * 128:tc * 128 + 128, :]),
                wait=wxr)
            wmm = [("s", a3_done_s)] + (wo_lms if tc == 0 else [])
            if vst_prev:
                wmm.append(("v", vst_prev))
            for og in range(4):
                for kc in range(16):
                    E("tensor", lambda e, tc=tc, og=og, kc=kc: e.matmul(
                        bank(og), ao_sb[:, kc, tc * 128:tc * 128 + 128],
                        wo_sb[:, kc, og * 512:og * 512 + 512],
                        start=(kc == 0), stop=(kc == 15)),
                      wait=wmm if (og == 0 and kc == 0) else (),
                      inc=[("t", 1)] if (og == 3 and kc == 15) else ())
            tm = C("t")
            if pend is not None:
                y1t_last_t = y1t_emit(*pend)
            for og in range(4):
                wstt = [("t", tm), xm] if og == 0 else []
                if og == 0 and sst_prev:
                    wstt.append(("s", sst_prev))
                E("vector", lambda e, og=og, bu=bu: e.scalar_tensor_tensor(
                    y1pre[:, og * 512:og * 512 + 512],
                    xres[bu][:, og * 512:og * 512 + 512], 2.0, bank(og),
                    op0=Alu.mult, op1=Alu.add),
                  wait=wstt)
            E("vector", lambda e: e.drain())
            E("vector", lambda e: e.tensor_reduce(nmean[:], y1pre[:], axis=AX,
                                                  op=Alu.add, negate=True))
            E("vector", lambda e: e.drain())
            E("vector", lambda e: e.tensor_single_scalar(
                nmean[:], nmean[:], 1.0 / 2048, Alu.mult))
            vst = vbar()
            xres_used[bu] = vst
            vst_prev = vst
            E("scalar", lambda e: e.activation(sq2[:, 0:2048], y1pre[:], Act.Square,
                                               bias=nmean[:], accum_out=ssd[:]),
              wait=[("v", vst)])
            E("scalar", lambda e: e.drain())
            E("scalar", lambda e: e.activation(rstd[:], ssd[:], Act.Ln,
                                               scale=1.0 / 2048, bias=eps5[:]))
            E("scalar", lambda e: e.drain())
            E("scalar", lambda e: e.activation(rstd[:], rstd[:], Act.Exp, scale=-0.5))
            sst = sbar()
            sst_prev = sst
            wln = [("s", sst)]
            if y1t_last_t:
                wln.append(("t", y1t_last_t))
            E("vector", lambda e: e.tensor_scalar(
                P_sb[:], y1pre[:], nmean[:], rstd[:], op0=Alu.add, op1=Alu.mult),
              wait=wln)
            lnv = vbar()
            pend = (tc, lnv)
        y1t_last_t = y1t_emit(*pend)
        a4_done_v = C("v")

        gu_mm = [0, 0]
        inter_ev = [0, 0]
        dn_mm = [0, 0]
        y2_add = [0, 0]
        dn_last_t = 0
        first = True
        for sc in range(8):
            for nc2 in range(16):
                nch = sc * 16 + nc2
                par = nc2 % 2
                wld = ([("t", gu_mm[par])] if gu_mm[par]
                       else [("cc", 2), ("v", a4_done_v)])
                glm1 = DMA("sync", lambda e, nch=nch, par=par: e.dma_start(
                    out=gub[:, par * 2, :, :], in_=g_fu.ap()[nch]), wait=wld)
                glm2 = DMA("sync", lambda e, nch=nch, par=par: e.dma_start(
                    out=gub[:, par * 2 + 1, :, :], in_=u_fu.ap()[nch]))
                reg = par * 4
                wmm = [glm1, glm2]
                if inter_ev[par]:
                    wmm.append(("v", inter_ev[par]))
                if sc > 0 and nc2 < 2:
                    wmm += [("v", y2_add[0]), ("v", y2_add[1])]
                if sc == 0 and nc2 < 2:
                    wmm.append(("v", a4_done_v))
                for kc in range(16):
                    for th in range(2):
                        E("tensor", lambda e, par=par, kc=kc, th=th, reg=reg:
                          e.matmul(bank(reg + th), gub[:, par * 2, kc, :],
                                   y1T[:, kc, th * 512:th * 512 + 512],
                                   start=(kc == 0), stop=(kc == 15)),
                          wait=wmm if (kc == 0 and th == 0) else ())
                        E("tensor", lambda e, par=par, kc=kc, th=th, reg=reg:
                          e.matmul(bank(reg + 2 + th), gub[:, par * 2 + 1, kc, :],
                                   y1T[:, kc, th * 512:th * 512 + 512],
                                   start=(kc == 0), stop=(kc == 15)),
                          inc=[("t", 1)] if (kc == 15 and th == 1) else ())
                gu_mm[par] = C("t")
                wge = [("t", gu_mm[par])]
                if inter_ev[par]:
                    wge.append(("v", inter_ev[par]))
                E("scalar", lambda e, par=par, reg=reg: e.activation(
                    sq2[:, par * 1024:par * 1024 + 1024], bank(reg, 2),
                    Act.Gelu_apprx_tanh), wait=wge)
                sg = sbar()
                wiv = [("s", sg)]
                if dn_last_t:
                    wiv.append(("t", dn_last_t))
                inter_ev[par] = E("vector", lambda e, nc2=nc2, par=par, reg=reg:
                                  e.tensor_tensor(
                                      inter[:, nc2, :], bank(reg + 2, 2),
                                      sq2[:, par * 1024:par * 1024 + 1024], Alu.mult),
                                  wait=wiv, inc=[("v", 1)])
            iv = C("v")
            for hc in range(16):
                par = hc % 2
                wld = [("t", dn_mm[par])] if dn_mm[par] else [("cc", ag_done),
                                                             ("v", a4_done_v)]
                dlm = DMA("sync", lambda e, sc=sc, hc=hc, par=par: e.dma_start(
                    out=dnb[:, par, :, :], in_=d_fu.ap()[sc * 16 + hc]),
                    wait=wld)
                reg = par * 2
                wmm = [dlm, ("v", iv)]
                if y2_add[par]:
                    wmm.append(("v", y2_add[par]))
                for nc2 in range(16):
                    for th in range(2):
                        E("tensor", lambda e, par=par, nc2=nc2, th=th, reg=reg:
                          e.matmul(bank(reg + th), dnb[:, par, nc2, :],
                                   inter[:, nc2, th * 512:th * 512 + 512],
                                   start=(nc2 == 0), stop=(nc2 == 15)),
                          wait=wmm if (nc2 == 0 and th == 0) else (),
                          inc=[("t", 1)] if (nc2 == 15 and th == 1) else ())
                dn_mm[par] = C("t")
                if first and hc < 2:
                    y2_add[par] = E("vector", lambda e, hc=hc, reg=reg: e.tensor_copy(
                        y2acc[:, hc, :], bank(reg, 2)),
                        wait=[("t", dn_mm[par])], inc=[("v", 1)])
                elif first:
                    y2_add[par] = E("vector", lambda e, hc=hc, reg=reg: e.tensor_copy(
                        y2acc[:, hc, :], bank(reg, 2)),
                        wait=[("t", dn_mm[par]), ("v", y2_add[par])], inc=[("v", 1)])
                else:
                    y2_add[par] = E("vector", lambda e, hc=hc, reg=reg:
                                    e.scalar_tensor_tensor(
                                        y2acc[:, hc, :], bank(reg, 2), 1.0,
                                        y2acc[:, hc, :], op0=Alu.mult, op1=Alu.add),
                                    wait=[("t", dn_mm[par]), ("v", y2_add[par])],
                                    inc=[("v", 1)])
            dn_last_t = C("t")
            first = False
        ffn_done_v = C("v")

        for kc in range(16):
            E("vector", lambda e, kc=kc: e.tensor_tensor(
                y2acc[:, kc, :], y2acc[:, kc, :], y1T[:, kc, :], Alu.add),
              wait=[("v", ffn_done_v)] if kc == 0 else ())
        E("vector", lambda e: e.drain())
        for kc in range(16):
            E("vector", lambda e, kc=kc: e.tensor_copy(y1T[:, kc, :], y2acc[:, kc, :]))
        vy2b = vbar()
        E("scalar", lambda e: e.activation(ar2[:], ar3[:], Act.Square),
          wait=[("v", vy2b)])
        ssq = sbar()
        st_ms = [0, 0]
        vvar_prev = 0
        for tb in range(2):
            w0 = [("s", ssq)] if tb == 0 else [("v", vvar_prev)]
            for kc in range(16):
                E("tensor", lambda e, tb=tb, kc=kc: e.matmul(
                    bank(0)[0:1, 0:512], ones_b[:],
                    y1T[:, kc, tb * 512:tb * 512 + 512],
                    start=(kc == 0), stop=(kc == 15)),
                  wait=w0 if kc == 0 else ())
            for kc in range(16):
                E("tensor", lambda e, tb=tb, kc=kc: e.matmul(
                    bank(1)[0:1, 0:512], ones_b[:],
                    inter[:, kc, tb * 512:tb * 512 + 512],
                    start=(kc == 0), stop=(kc == 15)),
                  inc=[("t", 1)] if kc == 15 else ())
            tsum = C("t")
            E("vector", lambda e: e.tensor_single_scalar(
                ln2a[:], bank(0)[0:1, 0:512], -1.0 / 2048, Alu.mult),
              wait=[("t", tsum)])
            E("vector", lambda e: e.tensor_single_scalar(
                ln2b[:], bank(1)[0:1, 0:512], 1.0 / 2048, Alu.mult))
            E("vector", lambda e: e.drain())
            E("vector", lambda e: e.tensor_tensor(ln2c[:], ln2a[:], ln2a[:], Alu.mult))
            E("vector", lambda e: e.drain())
            E("vector", lambda e: e.tensor_tensor(ln2b[:], ln2b[:], ln2c[:],
                                                  Alu.subtract))
            vvar_prev = vvar = vbar()
            E("scalar", lambda e: e.activation(ln2c[:], ln2b[:], Act.Ln,
                                               bias=eps5[0:1, :]),
              wait=[("v", vvar)])
            E("scalar", lambda e: e.drain())
            E("scalar", lambda e: e.activation(ln2c[:], ln2c[:], Act.Exp, scale=-0.5))
            srs = sbar()
            wbc = [("s", srs)]
            if st_ms[0]:
                wbc.append(("v", st_ms[0]))
            E("tensor", lambda e: e.matmul(bank(2), ones_f[:], ln2a[:],
                                           start=True, stop=True), wait=wbc)
            E("tensor", lambda e: e.matmul(bank(3), ones_f[:], ln2c[:],
                                           start=True, stop=True), inc=[("t", 1)])
            tbc = C("t")
            for kc in range(16):
                E("vector", lambda e, tb=tb, kc=kc: e.tensor_tensor(
                    y2acc[:, kc, tb * 512:tb * 512 + 512],
                    y2acc[:, kc, tb * 512:tb * 512 + 512], bank(2), Alu.add),
                  wait=[("t", tbc)] if kc == 0 else ())
            E("vector", lambda e: e.drain())
            for kc in range(16):
                E("vector", lambda e, tb=tb, kc=kc: e.tensor_tensor(
                    y2acc[:, kc, tb * 512:tb * 512 + 512],
                    y2acc[:, kc, tb * 512:tb * 512 + 512], bank(3), Alu.mult))
            st_ms[tb] = vbar()
        ev_fin = 0
        st_fin = [0, 0]
        for tc in range(8):
            bu = tc % 2
            wt = [("v", st_ms[tc // 4])]
            if ev_fin:
                wt.append(("v", ev_fin))
            for kc in range(16):
                bk = 4 + kc // 4
                E("tensor", lambda e, tc=tc, kc=kc, bk=bk: e.transpose(
                    bank(bk)[:, (kc % 4) * 128:(kc % 4) * 128 + 128],
                    y2acc[:, kc, tc * 128:tc * 128 + 128], idf_sb[:]),
                  wait=wt if kc == 0 else (),
                  inc=[("t", 1)] if kc == 15 else ())
            tfin = C("t")
            wev = [("t", tfin)]
            if st_fin[bu]:
                wev.append(st_fin[bu])
            for bk in range(4):
                E("vector", lambda e, bu=bu, bk=bk: e.tensor_copy(
                    y2tok[bu][:, bk * 512:bk * 512 + 512].rearrange(
                        "p (a j two) -> p a two j", a=4, two=2),
                    bank(4 + bk)[:].rearrange("p (a b) -> p a b", b=128).rearrange(
                        "p a (two j) -> p a two j", two=2)),
                  wait=wev if bk == 0 else ())
            ev_fin = vbar()
            st_fin[bu] = DMA("gpsimd", lambda e, tc=tc, bu=bu: e.dma_start(
                out=y_h[tc * 128:tc * 128 + 128, :], in_=y2tok[bu][:]),
                wait=[("v", ev_fin)])
        E("gpsimd", lambda e: e.nop(),
          wait=(bld.dma_totals("gpsimd") + bld.dma_totals("sync")
                + bld.dma_totals("scalar")))

        for ename in ENGS:
            entries = bld.entries[ename]

            def runner(eng, _entries=entries):
                for wait, f, incs in _entries:
                    for s_, v_ in wait:
                        eng.wait_ge(SEM[s_], v_)
                    ins = f(eng)
                    for s_, n_ in incs:
                        ins.then_inc(SEM[s_], n_)
            getattr(blk, ename)(runner)

    n_inst = sum(len(v) for v in bld.entries.values())
    print(f"[kernel] emitted {n_inst} entries "
          + str({k: len(v) for k, v in bld.entries.items()}))
    return nc


def kernel(**inputs):
    x = np.asarray(inputs["L0_attn_x"], dtype=np.float32)
    wq = np.asarray(inputs["L0_attn_Wq"], dtype=np.float32)
    wk = np.asarray(inputs["L0_attn_Wk"], dtype=np.float32)
    wv = np.asarray(inputs["L0_attn_Wv"], dtype=np.float32)
    wo = np.asarray(inputs["L0_attn_Wo"], dtype=np.float32)
    qg = np.ascontiguousarray(
        np.asarray(inputs["L0_attn_q_gamma"], dtype=np.float32).reshape(128, 1))
    kg = np.ascontiguousarray(
        np.asarray(inputs["L0_attn_k_gamma"], dtype=np.float32).reshape(128, 1))
    gp = np.asarray(inputs["L0_ffn_gate_W_packed"], dtype=np.int32)
    gs = np.asarray(inputs["L0_ffn_gate_scales"], dtype=np.float32)
    up = np.asarray(inputs["L0_ffn_up_W_packed"], dtype=np.int32)
    us = np.asarray(inputs["L0_ffn_up_scales"], dtype=np.float32)
    dp = np.asarray(inputs["L0_ffn_down_W_packed"], dtype=np.int32)
    ds = np.asarray(inputs["L0_ffn_down_scales"], dtype=np.float32)

    ident = np.eye(128, dtype=ml_dtypes.bfloat16)
    identf = np.eye(128, dtype=np.float32)

    in_maps = []
    for c in range(NCORES):
        b, par = c // 2, c % 2
        blocks = QBLOCKS[par]
        xq_rows = np.concatenate([x[b, g * 128:(g + 1) * 128, :] for g in blocks], 0)
        mk = np.zeros((8, 128, 512), dtype=np.float32)
        for qb, g in enumerate(blocks):
            L5 = ((MPAT[qb] + 1 + 3) // 4) * 512
            k0 = L5 - 512
            kk = k0 + np.arange(512)[None, :]
            qq = g * 128 + np.arange(128)[:, None]
            mk[qb] = np.where(kk <= qq, 0.0, NEG)
        in_maps.append({
            "x_full": np.ascontiguousarray(x[b]),
            "x_q": np.ascontiguousarray(xq_rows),
            "wq": wq, "wk": wk, "wv": wv, "wo": wo,
            "gp": np.ascontiguousarray(gp[c * 2048:(c + 1) * 2048]),
            "gs": np.ascontiguousarray(gs[c * 2048:(c + 1) * 2048]),
            "up": np.ascontiguousarray(up[c * 2048:(c + 1) * 2048]),
            "us": np.ascontiguousarray(us[c * 2048:(c + 1) * 2048]),
            "dp": np.ascontiguousarray(dp[:, c * 1024:(c + 1) * 1024]),
            "ds": np.ascontiguousarray(ds[:, c * 64:(c + 1) * 64]),
            "qg": qg, "kg": kg, "ident": ident, "identf": identf, "maskqb": mk,
        })

    nc = build_nc()
    res = run_bass_kernel_spmd(nc, in_maps, core_ids=list(range(NCORES)))

    out = np.empty((B, S, D), dtype=np.float32)
    for c in range(NCORES):
        b, par = c // 2, c % 2
        y = res.results[c]["y"]
        for qb, g in enumerate(QBLOCKS[par]):
            out[b, g * 128:(g + 1) * 128, :] = y[qb * 128:(qb + 1) * 128, :]
    return out


# version 22
# speedup vs baseline: 1.0702x; 1.0702x over previous
import contextlib
import numpy as np
import ml_dtypes
import concourse.bass as bass
import concourse.mybir as mybir
from concourse.bass_utils import run_bass_kernel_spmd

F32 = mybir.dt.float32
BF16 = mybir.dt.bfloat16
I32 = mybir.dt.int32
I16 = mybir.dt.int16
Alu = mybir.AluOpType
Act = mybir.ActivationFunctionType
AX = mybir.AxisListType.X

B, S, D = 4, 2048, 2048
H, KVH, DH = 16, 4, 128
NCORES = 8
NEG = -1e30
TQ = 1024

QBLOCKS = [[0, 3, 4, 7, 8, 11, 12, 15], [1, 2, 5, 6, 9, 10, 13, 14]]
MPAT = [1, 3, 5, 7, 9, 11, 13, 15]

ENGS = ("sync", "tensor", "vector", "scalar", "gpsimd")


NSLOT = 8


class Build:

    def __init__(self):
        self.entries = {e: [] for e in ENGS}
        self.cnt = {}
        self.slot_i = {"sync": 0, "gpsimd": 0, "scalar": 0}

    def emit(self, eng, f, wait=(), inc=()):
        self.entries[eng].append((tuple(wait), f, tuple(inc)))
        out = None
        for s, n in inc:
            self.cnt[s] = self.cnt.get(s, 0) + n
            out = self.cnt[s]
        return out

    QPFX = {"sync": "ld", "gpsimd": "gd", "scalar": "sd"}

    def dma(self, eng, f, wait=()):
        q = self.QPFX[eng]
        i = self.slot_i[eng] % NSLOT
        self.slot_i[eng] += 1
        s = f"{q}{i}"
        uses = self.cnt.get(s, 0)
        w = list(wait)
        if uses:
            w.append((s, uses))
        v = self.emit(eng, f, wait=w, inc=[(s, 16)])
        return (s, v)

    def dma_totals(self, eng):
        q = self.QPFX[eng]
        return [(f"{q}{i}", self.cnt.get(f"{q}{i}", 0)) for i in range(NSLOT)
                if self.cnt.get(f"{q}{i}", 0)]

    def cur(self, s):
        return self.cnt.get(s, 0)


def build_nc():
    nc = bass.Bass("TRN2", target_bir_lowering=False, debug=False)
    KJ, NB = 1024, 64

    x_full = nc.dram_tensor("x_full", [S, D], F32, kind="ExternalInput").ap()
    x_q = nc.dram_tensor("x_q", [TQ, D], F32, kind="ExternalInput").ap()
    wq_h = nc.dram_tensor("wq", [D, D], F32, kind="ExternalInput").ap()
    wk_h = nc.dram_tensor("wk", [D, KVH * DH], F32, kind="ExternalInput").ap()
    wv_h = nc.dram_tensor("wv", [D, KVH * DH], F32, kind="ExternalInput").ap()
    wo_h = nc.dram_tensor("wo", [D, D], F32, kind="ExternalInput").ap()
    gp_h = nc.dram_tensor("gp", [2048, KJ], I16, kind="ExternalInput").ap()
    gs_h = nc.dram_tensor("gs", [2048, NB], F32, kind="ExternalInput").ap()
    up_h = nc.dram_tensor("up", [2048, KJ], I16, kind="ExternalInput").ap()
    us_h = nc.dram_tensor("us", [2048, NB], F32, kind="ExternalInput").ap()
    dp_h = nc.dram_tensor("dp", [2048, KJ], I16, kind="ExternalInput").ap()
    ds_h = nc.dram_tensor("ds", [2048, NB], F32, kind="ExternalInput").ap()
    qg_h = nc.dram_tensor("qg", [128, 1], F32, kind="ExternalInput").ap()
    kg_h = nc.dram_tensor("kg", [128, 1], F32, kind="ExternalInput").ap()
    id_h = nc.dram_tensor("ident", [128, 128], BF16, kind="ExternalInput").ap()
    idf_h = nc.dram_tensor("identf", [128, 128], F32, kind="ExternalInput").ap()
    mk_h = nc.dram_tensor("maskqb", [8, 128, 512], F32, kind="ExternalInput").ap()
    y_h = nc.dram_tensor("y", [TQ, D], F32, kind="ExternalOutput").ap()

    wqb_h = nc.dram_tensor("wq_bf", [D, D], BF16).ap()
    wkb_h = nc.dram_tensor("wk_bf", [D, KVH * DH], BF16).ap()
    wvb_h = nc.dram_tensor("wv_bf", [D, KVH * DH], BF16).ap()
    wob_h = nc.dram_tensor("wo_bf", [D, D], BF16).ap()
    g_sh = nc.dram_tensor("g_sh", [16, 128, 16, 128], BF16)
    u_sh = nc.dram_tensor("u_sh", [16, 128, 16, 128], BF16)
    d_sh = nc.dram_tensor("d_sh", [16, 128, 16, 128], BF16)
    g_fu = nc.dram_tensor("g_fu", [128, 128, 16, 128], BF16)
    u_fu = nc.dram_tensor("u_fu", [128, 128, 16, 128], BF16)
    d_fu = nc.dram_tensor("d_fu", [128, 128, 16, 128], BF16)

    bld = Build()
    E, C = bld.emit, bld.cur

    def vbar():
        return E("vector", lambda e: e.drain(), inc=[("v", 1)])

    def sbar():
        return E("scalar", lambda e: e.drain(), inc=[("s", 1)])

    with contextlib.ExitStack() as ST:
        alloc = ST.enter_context

        ar1 = alloc(nc.sbuf_tensor("ar1", [128, 32768], BF16))
        ar2 = alloc(nc.sbuf_tensor("ar2", [128, 16384], BF16))
        ar3 = alloc(nc.sbuf_tensor("ar3", [128, 16384], BF16))
        ar4 = alloc(nc.sbuf_tensor("ar4", [128, 8192], BF16))
        ar5 = alloc(nc.sbuf_tensor("ar5", [128, 8192], BF16))
        ar6 = alloc(nc.sbuf_tensor("ar6", [128, 8192], BF16))
        ar7 = alloc(nc.sbuf_tensor("ar7", [128, 8192], BF16))
        arp = alloc(nc.sbuf_tensor("arp", [128, 4096], BF16))
        id_sb = alloc(nc.sbuf_tensor("id_sb", [128, 128], BF16))
        idf_sb = alloc(nc.sbuf_tensor("idf_sb", [128, 128], F32))
        qg_sb = alloc(nc.sbuf_tensor("qg_sb", [128, 1], F32))
        kg_sb = alloc(nc.sbuf_tensor("kg_sb", [128, 1], F32))
        ones_b = alloc(nc.sbuf_tensor("ones_b", [128, 1], BF16))
        ones_f = alloc(nc.sbuf_tensor("ones_f", [1, 128], F32))
        nmax = alloc(nc.sbuf_tensor("nmax", [128, 1], F32))
        rsum = alloc(nc.sbuf_tensor("rsum", [128, 1], F32))
        rcp = alloc(nc.sbuf_tensor("rcp", [128, 1], F32))
        nmean = alloc(nc.sbuf_tensor("nmean", [128, 1], F32))
        ssd = alloc(nc.sbuf_tensor("ssd", [128, 1], F32))
        rstd = alloc(nc.sbuf_tensor("rstd", [128, 1], F32))
        sq2 = alloc(nc.sbuf_tensor("sq2", [128, 2048], BF16))
        eps8 = alloc(nc.sbuf_tensor("eps8", [128, 1], F32))
        eps5 = alloc(nc.sbuf_tensor("eps5", [128, 1], F32))
        ps = alloc(nc.psum_tensor("ps", [128, 4096], F32))

        SEM = {}
        semnames = ["t", "v", "s", "cc", "wc"]
        semnames += [f"ld{i}" for i in range(NSLOT)] + [f"gd{i}" for i in range(NSLOT)]
        semnames += [f"sd{i}" for i in range(NSLOT)]
        for s_ in semnames:
            SEM[s_] = alloc(nc.semaphore(s_))
        blk = alloc(nc.Block())

        def bank(i, n=1):
            return ps[:, 512 * i:512 * (i + n)]

        def bank_bf(i, n=1):
            return ps[:, 512 * i:512 * (i + n)].bitcast(BF16)

        xfT = ar1[:].rearrange("p (a b) -> p a b", b=2048)
        wo_sb = ar1[:].rearrange("p (a b) -> p a b", b=2048)
        y2acc = ar1[:].bitcast(F32).rearrange("p (a b) -> p a b", b=1024)
        xqT = ar2[:].rearrange("p (a b) -> p a b", b=1024)
        wv_sb = ar2[:, 8192:16384].rearrange("p (a b) -> p a b", b=512)
        ao_sb = ar2[:].rearrange("p (a b) -> p a b", b=1024)
        inter = ar2[:].rearrange("p (a b) -> p a b", b=1024)
        qhat = ar3[:].rearrange("p (a b) -> p a b", b=1024)
        y1T = ar3[:].rearrange("p (a b) -> p a b", b=1024)
        khat = ar4[:].rearrange("p (a b) -> p a b", b=2048)
        gub = ar4[:].rearrange("p (a b c) -> p a b c", b=16, c=128)
        v_sb = ar5[:].rearrange("p (a b) -> p a b", b=512)
        dnb = ar5[:, 0:4096].rearrange("p (a b c) -> p a b c", b=16, c=128)
        y1pre = ar5[:, 0:4096].bitcast(F32)
        dq_b32 = [ar3[:, i * 1024:(i + 1) * 1024].bitcast(I16) for i in range(6)]
        dq_m = arp[:, 0:1024].bitcast(I16)
        dq_m2 = arp[:, 1024:2048].bitcast(I16)
        dq_bb = arp[:, 2048:3072].bitcast(I16)
        dq_bbh = arp[:, 3072:4096].bitcast(I16)
        dq_s2 = sq2[:, 0:1024].bitcast(I16)
        dq_s2h = sq2[:, 1024:2048].bitcast(I16)
        dq_wraw = [ar5[:, 5120:7168], ar4[:, 6144:8192]]
        dq_bits = [w.bitcast(I16).rearrange("p (a b) -> p a b", b=1024) for w in dq_wraw]
        dq_w = [w.rearrange("p (a b) -> p a b", b=1024) for w in dq_wraw]
        dq_sc = [ar5[:, i * 128:(i + 1) * 128].bitcast(F32) for i in range(6)]
        dq_st = [ar5[:, 1024:3072].rearrange("p (a b) -> p a b", b=128),
                 ar5[:, 3072:5120].rearrange("p (a b) -> p a b", b=128)]
        masks = ar7[:].bitcast(F32).rearrange("p (a b) -> p a b", b=512)
        pcs_f = [ar4[:, i * 2048:(i + 1) * 2048].bitcast(F32) for i in range(3)]
        pcs_b = [ar3[:, 14336:15360], ar3[:, 15360:16384], ar5[:, 7168:8192]]
        xf32 = [ar6[:, i * 2048:(i + 1) * 2048].bitcast(F32) for i in range(4)]
        xres = [ar6[:, 0:4096].bitcast(F32), ar6[:, 4096:8192].bitcast(F32)]
        y2tok = [ar6[:, 0:4096].bitcast(F32), ar6[:, 4096:8192].bitcast(F32)]
        ln2a = arp[:, 0:1024].bitcast(F32)[0:1, 0:512]
        ln2b = arp[:, 1024:2048].bitcast(F32)[0:1, 0:512]
        ln2c = arp[:, 2048:3072].bitcast(F32)[0:1, 0:512]
        P_sb = arp[:, 0:2048]
        PT_sb = arp[:, 2048:4096].rearrange("p (a b) -> p a b", b=128)

        DMA = bld.dma
        id_ms = DMA("sync", lambda e: e.dma_start(out=id_sb[:], in_=id_h))
        idf_ms = DMA("sync", lambda e: e.dma_start(out=idf_sb[:], in_=idf_h))
        qg_ms = DMA("sync", lambda e: e.dma_start(out=qg_sb[:], in_=qg_h))
        kg_ms = DMA("sync", lambda e: e.dma_start(out=kg_sb[:], in_=kg_h))
        for qb in range(8):
            DMA("sync", lambda e, qb=qb: e.dma_start(out=masks[:, qb, :], in_=mk_h[qb]))
        const_ms = bld.dma_totals("sync")
        E("vector", lambda e: e.memset(ones_b[:], 1.0))
        E("vector", lambda e: e.memset(ones_f[:], 1.0))
        E("vector", lambda e: e.memset(eps8[:], 1e-8))
        E("vector", lambda e: e.memset(eps5[:], 1e-5))
        E("vector", lambda e: e.drain())

        pc_items = []
        for kc in range(16):
            for hf in range(2):
                pc_items.append((wq_h[kc * 128:kc * 128 + 128,
                                      hf * 1024:hf * 1024 + 1024],
                                 wqb_h[kc * 128:kc * 128 + 128,
                                       hf * 1024:hf * 1024 + 1024], 1024))
        for kc in range(16):
            pc_items.append((wk_h[kc * 128:kc * 128 + 128, :],
                             wkb_h[kc * 128:kc * 128 + 128, :], 512))
            pc_items.append((wv_h[kc * 128:kc * 128 + 128, :],
                             wvb_h[kc * 128:kc * 128 + 128, :], 512))
        for kc in range(16):
            for hf in range(2):
                pc_items.append((wo_h[kc * 128:kc * 128 + 128,
                                      hf * 1024:hf * 1024 + 1024],
                                 wob_h[kc * 128:kc * 128 + 128,
                                       hf * 1024:hf * 1024 + 1024], 1024))
        pc_lm = {}
        pc_sm = [0, 0, 0]

        def pc_load(it):
            if it >= len(pc_items):
                return
            src_, dst_, w_ = pc_items[it]
            sl = it % 3
            wld = [pc_sm[sl]] if pc_sm[sl] else []
            pc_lm[it] = DMA("gpsimd", lambda e, src_=src_, sl=sl, w_=w_:
                            e.dma_start(out=pcs_f[sl][:, 0:w_], in_=src_),
                            wait=wld)

        def pc_proc(i0):
            for it in (i0, i0 + 1):
                sl = it % 3
                w_ = pc_items[it][2]
                E("scalar", lambda e, sl=sl, w_=w_: e.activation(
                    pcs_b[sl][:, 0:w_], pcs_f[sl][:, 0:w_], Act.Identity),
                  wait=[pc_lm[it]])
            pcst = sbar()
            for it in (i0, i0 + 1):
                sl = it % 3
                dst_, w_ = pc_items[it][1], pc_items[it][2]
                pc_sm[sl] = DMA("sync", lambda e, dst_=dst_, sl=sl, w_=w_:
                                e.dma_start(out=dst_, in_=pcs_b[sl][:, 0:w_]),
                                wait=[("s", pcst)])

        def dq_chain(v, b6, t2):
            b16v = dq_b32[b6]
            v.tensor_single_scalar(dq_m[:], b16v[:], 7, Alu.bitwise_and)
            v.tensor_single_scalar(dq_m2[:], b16v[:], 0x70, Alu.bitwise_and)
            v.tensor_single_scalar(dq_s2[:], b16v[:], 8, Alu.bitwise_and)
            v.tensor_single_scalar(dq_s2h[:], b16v[:], 128, Alu.bitwise_and)
            v.tensor_scalar(dq_bb[:], dq_m[:], 2, 16128, op0=Alu.min, op1=Alu.mult)
            v.tensor_scalar(dq_bbh[:], dq_m2[:], 32, 1008, op0=Alu.min, op1=Alu.mult)
            v.tensor_single_scalar(dq_s2[:], dq_s2[:], -4096, Alu.mult)
            v.tensor_single_scalar(dq_s2h[:], dq_s2h[:], -256, Alu.mult)
            v.tensor_single_scalar(dq_m[:], dq_m[:], 64, Alu.mult)
            v.tensor_single_scalar(dq_m2[:], dq_m2[:], 4, Alu.mult)
            v.scalar_tensor_tensor(dq_m[:], dq_m[:], 16128.0, dq_bb[:],
                                   op0=Alu.add, op1=Alu.min)
            v.scalar_tensor_tensor(dq_m2[:], dq_m2[:], 16128.0, dq_bbh[:],
                                   op0=Alu.add, op1=Alu.min)
            v.tensor_tensor(dq_bits[t2][:, 0, :], dq_m[:], dq_s2[:], Alu.add)
            v.tensor_tensor(dq_bits[t2][:, 1, :], dq_m2[:], dq_s2h[:], Alu.add)
            sc_b = dq_sc[b6][:].unsqueeze(2).broadcast_to([128, 64, 16])
            for pl in range(2):
                bits_bf = dq_bits[t2][:, pl, :].bitcast(BF16).rearrange(
                    "n (b u) -> n b u", u=16)
                v.tensor_tensor(dq_w[t2][:, pl, :].rearrange(
                    "n (b u) -> n b u", u=16), bits_bf, sc_b, Alu.mult)
            return v.drain()

        mats = ((gp_h, gs_h, g_sh), (up_h, us_h, u_sh), (dp_h, ds_h, d_sh))
        a1_items = ([(x_full, xfT, tt, hf) for tt in range(16) for hf in range(2)]
                    + [(x_q, xqT, tt, hf) for tt in range(8) for hf in range(2)])
        a1_lm = {}
        a1_tr = [0, 0, 0, 0]
        a1_ev = [0, 0]
        a1_g = 0

        def a1_load(hi):
            if hi >= len(a1_items):
                return
            xd, dst, tt, hf = a1_items[hi]
            sl = hi % 4
            wt0 = [("t", a1_tr[sl])] if a1_tr[sl] else []
            a1_lm[hi] = DMA("sync", lambda e, xd=xd, tt=tt, hf=hf, sl=sl:
                            e.dma_start(out=xf32[sl][:],
                                        in_=xd[tt * 128:tt * 128 + 128,
                                               hf * 1024:hf * 1024 + 1024]),
                            wait=wt0)

        dq_cons = [0] * 6

        def dq_load(g2):
            if g2 >= 48:
                return (None, None)
            mi2, oc2 = g2 // 16, g2 % 16
            pk2, scl2 = mats[mi2][0], mats[mi2][1]
            b62 = g2 % 6
            wld = []
            mm1 = DMA("scalar", lambda e, pk2=pk2, oc2=oc2, b62=b62: e.dma_start(
                out=dq_b32[b62][:],
                in_=pk2[oc2 * 128:oc2 * 128 + 128, :].rearrange(
                    "(j two) c -> two j c", two=2)), wait=wld)
            mm2 = DMA("scalar", lambda e, scl2=scl2, oc2=oc2, b62=b62: e.dma_start(
                out=dq_sc[b62][:],
                in_=scl2[oc2 * 128:oc2 * 128 + 128, :].rearrange(
                    "(j two) c -> two j c", two=2)))
            return (mm1, mm2)

        dq_lms = {}
        dq_tr = [0, 0]
        dq_evac = [0, 0]
        dq_store = [0, 0]
        dq_vm = 0
        a1_load(0)
        a1_load(1)
        a1_load(2)
        pc_load(0)
        pc_load(1)
        for g2 in range(6):
            dq_lms[g2] = dq_load(g2)
        for goc in range(48):
            mi, oc = goc // 16, goc % 16
            sh = mats[mi][2]
            t2 = goc % 2
            m1, m2 = dq_lms[goc]
            wv_ = [m1, m2]
            if dq_tr[t2]:
                wv_.append(("t", dq_tr[t2]))
            dq_vm = E("vector", lambda e, b6=goc % 6, t2=t2: dq_chain(e, b6, t2),
                      wait=wv_, inc=[("v", 1)])
            dq_cons[goc % 6] = dq_vm
            for bh in range(2):
                wt = [("v", dq_vm)]
                if dq_evac[bh]:
                    wt.append(("s", dq_evac[bh]))
                for kcl in range(8):
                    kc = bh * 8 + kcl
                    E("tensor", lambda e, bh=bh, kcl=kcl, kc=kc, t2=t2:
                      e.transpose(
                          bank_bf(6 + bh)[0:64, kcl * 128:kcl * 128 + 128],
                          dq_w[t2][:, 0, kc * 64:kc * 64 + 64], id_sb[:]),
                      wait=wt if kcl == 0 else ())
                    E("tensor", lambda e, bh=bh, kcl=kcl, kc=kc, t2=t2:
                      e.transpose(
                          bank_bf(6 + bh)[64:128, kcl * 128:kcl * 128 + 128],
                          dq_w[t2][:, 1, kc * 64:kc * 64 + 64], id_sb[:]),
                      inc=[("t", 1)] if kcl == 7 else ())
                tm = C("t")
                wev = [("t", tm)]
                if dq_store[t2]:
                    wev.append(dq_store[t2])
                dq_evac[bh] = E("scalar", lambda e, bh=bh, t2=t2: e.copy(
                    dq_st[t2][:, bh * 8:bh * 8 + 8, :],
                    bank_bf(6 + bh)[:].rearrange("p (a b) -> p a b", b=128)),
                    wait=wev, inc=[("s", 1)])
            dq_tr[t2] = C("t")
            dq_lms[goc + 6] = dq_load(goc + 6)
            dq_store[t2] = DMA("scalar", lambda e, sh=sh, oc=oc, t2=t2:
                               e.dma_start(out=sh.ap()[oc], in_=dq_st[t2][:]),
                               wait=[("s", dq_evac[1])])
            xd, dst, tt, hf = a1_items[goc]
            sl = goc % 4
            for sub in range(2):
                bk = 4 + (a1_g % 2)
                wt2 = [a1_lm[goc]] if sub == 0 else []
                if a1_ev[a1_g % 2]:
                    wt2.append(("s", a1_ev[a1_g % 2]))
                for q in range(4):
                    kcl = sub * 4 + q
                    E("tensor", lambda e, sl=sl, kcl=kcl, bk=bk, q=q:
                      e.transpose(
                          ps[:, bk * 512 + q * 128:bk * 512 + q * 128 + 128],
                          xf32[sl][:, kcl * 128:kcl * 128 + 128], idf_sb[:]),
                      wait=wt2 if q == 0 else (),
                      inc=[("t", 1)] if q == 3 else ())
                tm = C("t")
                if sub == 1:
                    a1_tr[sl] = tm
                a1_ev[a1_g % 2] = E(
                    "scalar",
                    lambda e, dst=dst, tt=tt, hf=hf, sub=sub, bk=bk: e.copy(
                        dst[:, hf * 8 + sub * 4:hf * 8 + sub * 4 + 4,
                            tt * 128:tt * 128 + 128],
                        ps[:, bk * 512:bk * 512 + 512].rearrange(
                            "p (a b) -> p a b", b=128)),
                    wait=[("t", tm)], inc=[("s", 1)])
                a1_g += 1
            a1_load(goc + 2)
            pc_proc(2 * goc)
            pc_load(2 * goc + 2)
            pc_load(2 * goc + 3)
        xT_done_v = ("s", C("s"))
        dq_vm_last = dq_vm
        t_ph0 = C("t")
        pc_fin = bld.dma_totals("sync") + bld.dma_totals("scalar")
        dq_store_last = [dq_store[0], dq_store[1]]
        ag_w = (bld.dma_totals("scalar") + bld.dma_totals("sync")
                + bld.dma_totals("gpsimd"))

        rms_ln = arp[:].bitcast(F32)[0:1, 0:2048]

        def rms_apply(raw_ap, out_aps, nblk, reg, gam, extra=(), bank_guard=0):
            sqv = sq2[:, 0:nblk * 512]
            E("scalar", lambda e: e.activation(sqv, raw_ap, Act.Square), wait=extra)
            sm = sbar()
            for b_ in range(nblk):
                w_ss = [("s", sm)] + ([("v", bank_guard)] if bank_guard else [])
                E("tensor", lambda e, b_=b_: e.matmul(
                    ps[0:1, (reg + b_) * 512:(reg + b_) * 512 + 512], ones_b[:],
                    sqv[:, b_ * 512:b_ * 512 + 512], start=True, stop=True),
                  wait=w_ss if b_ == 0 else (),
                  inc=[("t", 1)] if b_ == nblk - 1 else ())
            tm = C("t")
            E("scalar", lambda e: e.activation(
                rms_ln[:, 0:nblk * 512], ps[0:1, reg * 512:(reg + nblk) * 512],
                Act.Ln, scale=1.0 / 128, bias=eps8[0:1, :]), wait=[("t", tm)])
            E("scalar", lambda e: e.drain())
            E("scalar", lambda e: e.activation(rms_ln[:, 0:nblk * 512],
                                               rms_ln[:, 0:nblk * 512], Act.Exp,
                                               scale=-0.5))
            sm2 = sbar()
            for b_ in range(nblk):
                E("tensor", lambda e, b_=b_: e.matmul(
                    ps[:, (reg + b_) * 512:(reg + b_) * 512 + 512], ones_f[:],
                    rms_ln[:, b_ * 512:b_ * 512 + 512], start=True, stop=True),
                  wait=[("s", sm2)] if b_ == 0 else (),
                  inc=[("t", 1)] if b_ == nblk - 1 else ())
            tm2 = C("t")
            for b_ in range(nblk):
                E("vector", lambda e, b_=b_, gam=gam: e.scalar_tensor_tensor(
                    out_aps[b_], ps[:, (reg + b_) * 512:(reg + b_) * 512 + 512],
                    gam[:], raw_ap[:, b_ * 512:b_ * 512 + 512],
                    op0=Alu.mult, op1=Alu.mult),
                  wait=[("t", tm2)] if b_ == 0 else ())
            return vbar()

        wslab = [ar4[:, i * 1024:(i + 1) * 1024] for i in range(8)]
        slab_mm = [0] * 8
        bank_ev = [0] * 8
        qev = 0
        si = 0
        for thh in range(4):
            hh8, th = thh // 2, thh % 2
            for kc in range(16):
                s8 = si % 8
                wld = [("t", slab_mm[s8])] if slab_mm[s8] else (
                    list(pc_fin) + [("t", t_ph0)] if si == 0 else [])
                lm = DMA("sync", lambda e, hh8=hh8, kc=kc, s8=s8: e.dma_start(
                    out=wslab[s8][:],
                    in_=wqb_h[kc * 128:kc * 128 + 128,
                              hh8 * 1024:hh8 * 1024 + 1024]), wait=wld)
                wmm = [lm]
                if thh == 0 and kc == 0:
                    wmm += [xT_done_v, ("v", dq_vm_last)]
                for h8 in range(8):
                    wmh = list(wmm) if h8 == 0 else []
                    if kc == 0 and bank_ev[h8]:
                        wmh.append(("s", bank_ev[h8]))
                    E("tensor", lambda e, h8=h8, kc=kc, th=th, s8=s8: e.matmul(
                        bank(h8), wslab[s8][:, h8 * 128:h8 * 128 + 128],
                        xqT[:, kc, th * 512:th * 512 + 512],
                        start=(kc == 0), stop=(kc == 15)),
                      wait=wmh,
                      inc=[("t", 1)] if h8 == 7 else ())
                slab_mm[s8] = C("t")
                si += 1
            tq_ = C("t")
            for h8 in range(8):
                wq_ev = [("t", tq_)] if h8 == 0 else []
                if h8 == 0 and thh == 0:
                    wq_ev.append(("v", dq_vm_last))
                if h8 == 0 and thh == 2:
                    wq_ev += pc_fin
                bank_ev[h8] = E("scalar", lambda e, hh8=hh8, th=th, h8=h8: e.copy(
                    qhat[:, hh8 * 8 + h8, th * 512:th * 512 + 512], bank(h8)),
                  wait=wq_ev, inc=[("s", 1)])
            qev = C("s")
        q_all_t = C("t")
        wk_sb = ar6[:].rearrange("p (a b) -> p a b", b=512)
        wk_lm = DMA("sync", lambda e: e.dma_start(
            out=wk_sb[:], in_=wkb_h.rearrange("(c p) n -> p c n", p=128)),
            wait=[("t", t_ph0)])
        for th in range(2):
            for kc in range(16):
                for kv in range(4):
                    for t2 in range(2):
                        wmh = [wk_lm] if (kc == 0 and kv == 0 and t2 == 0) else []
                        if kc == 0 and bank_ev[kv * 2 + t2]:
                            wmh.append(("s", bank_ev[kv * 2 + t2]))
                        E("tensor", lambda e, kv=kv, t2=t2, kc=kc, th=th:
                          e.matmul(bank(kv * 2 + t2),
                                   wk_sb[:, kc, kv * 128:kv * 128 + 128],
                                   xfT[:, kc, th * 1024 + t2 * 512:
                                       th * 1024 + t2 * 512 + 512],
                                   start=(kc == 0), stop=(kc == 15)),
                          wait=wmh,
                          inc=[("t", 1)] if (kv == 3 and t2 == 1) else ())
            tk_ = C("t")
            for kv in range(4):
                bank_ev[kv * 2] = bank_ev[kv * 2 + 1] = E(
                    "scalar", lambda e, kv=kv, th=th: e.copy(
                        khat[:, kv, th * 1024:th * 1024 + 1024], bank(kv * 2, 2)),
                    wait=[("t", tk_)] if kv == 0 else (),
                    inc=[("s", 1)])
            qev = C("s")
        wv_lm = DMA("sync", lambda e: e.dma_start(
            out=wv_sb[:], in_=wvb_h.rearrange("(c p) n -> p c n", p=128)),
            wait=[("t", q_all_t)])
        rm = 0
        prev_rm = [("s", qev)]
        reg_guard = {0: 0, 4: 0}
        for wi in range(12):
            reg = (wi % 2) * 4
            if wi < 8:
                outs_ = [qhat[:, wi * 2 + b_ // 2, (b_ % 2) * 512:(b_ % 2) * 512 + 512]
                         for b_ in range(4)]
                raw = ar3[:, (wi * 2) * 1024:(wi * 2 + 2) * 1024]
                rm = rms_apply(raw, outs_, 4, reg, qg_sb, extra=prev_rm,
                               bank_guard=reg_guard[reg])
            else:
                kv = wi - 8
                outs_ = [khat[:, kv, b_ * 512:b_ * 512 + 512] for b_ in range(4)]
                raw = ar4[:, kv * 2048:(kv + 1) * 2048]
                rm = rms_apply(raw, outs_, 4, reg, kg_sb, extra=prev_rm,
                               bank_guard=reg_guard[reg])
            reg_guard[reg] = rm
            prev_rm = []
        kv_done = rm
        v_ev = [0, 0]
        for tt in range(16):
            reg = tt % 2
            wmm = [wv_lm, ("v", kv_done)] if tt < 2 else []
            if v_ev[reg]:
                wmm.append(("s", v_ev[reg]))
            for kc in range(16):
                E("tensor", lambda e, tt=tt, kc=kc, reg=reg: e.matmul(
                    bank(reg), xfT[:, kc, tt * 128:tt * 128 + 128], wv_sb[:, kc, :],
                    start=(kc == 0), stop=(kc == 15)),
                  wait=wmm if kc == 0 else (),
                  inc=[("t", 1)] if kc == 15 else ())
            vt = C("t")
            wve = [("t", vt)]
            if tt == 0:
                wve += dq_store_last
            v_ev[reg] = E("scalar", lambda e, tt=tt, reg=reg: e.copy(
                v_sb[:, tt, :], bank(reg)), wait=wve, inc=[("s", 1)])
        v_done = ("s", C("s"))
        ag_w = ag_w + [v_done]
        for sh_, fu_ in ((g_sh, g_fu), (u_sh, u_fu), (d_sh, d_fu)):
            E("gpsimd", lambda e, sh_=sh_, fu_=fu_: e.collective_compute(
                "AllGather", Alu.bypass, replica_groups=[list(range(NCORES))],
                ins=[sh_.ap().opt()], outs=[fu_.ap().opt()]),
              wait=ag_w, inc=[("cc", 1)])
            ag_w = ()
        ag_done = C("cc")


        for kc in range(16):
            DMA("sync", lambda e, kc=kc: e.dma_start(
                out=wo_sb[:, kc, :], in_=wob_h[kc * 128:kc * 128 + 128, :]),
                wait=[v_done] if kc == 0 else [])
        wo_lms = bld.dma_totals("sync")

        groups = [(qb, h) for qb in range(8) for h in range(H)]
        ng = len(groups)

        def geo(qb):
            M = MPAT[qb]
            nkt = M + 1
            ngr = (nkt + 3) // 4
            return nkt, ngr, ngr * 512

        def region(qb, h):
            ngr = geo(qb)[1]
            if ngr <= 2:
                return (h % 2) * 2
            if ngr == 3:
                return (h % 2) * 3
            return 0

        vsc_prev = 0
        bank_guard = [0, 0, 0, 0, 0, 0]
        pt_t_by_par = [0, 0]
        pt_ev = 0
        pv_ev = 0
        pvap = ps[:, 3584:3712]
        Pbuf = [P_sb, sq2]

        def emit_S(i):
            qb, h = groups[i]
            kv = h // 4
            nkt, ngr, L5 = geo(qb)
            r0 = region(qb, h)
            w = []
            gmax = max(bank_guard[r0:r0 + ngr])
            if gmax:
                w.append(("s", gmax))
            if i == 0:
                w = [v_done] + const_ms + w
            for g in range(ngr):
                E("tensor", lambda e, qb=qb, h=h, kv=kv, g=g, r0=r0: e.matmul(
                    bank(r0 + g), qhat[:, h, qb * 128:qb * 128 + 128],
                    khat[:, kv, g * 512:g * 512 + 512], start=True, stop=True),
                  wait=w if g == 0 else (),
                  inc=[("t", 1)] if g == ngr - 1 else ())
            return C("t")

        tS_ms = emit_S(0)
        for i in range(ng):
            qb, h = groups[i]
            kv = h // 4
            nkt, ngr, L5 = geo(qb)
            r0 = region(qb, h)
            Sap = ps[:, r0 * 512:r0 * 512 + L5]
            Pv = Pbuf[i % 2]
            E("vector", lambda e, qb=qb, Sap=Sap, L5=L5: e.tensor_tensor(
                Sap[:, L5 - 512:L5], Sap[:, L5 - 512:L5], masks[:, qb, :], Alu.add),
              wait=[("t", tS_ms)])
            E("vector", lambda e: e.drain())
            E("vector", lambda e, Sap=Sap: e.tensor_reduce(
                nmax[:], Sap, axis=AX, op=Alu.max, negate=True))
            E("vector", lambda e: e.drain())
            E("vector", lambda e: e.tensor_single_scalar(
                nmax[:], nmax[:], 0.08838834764831845, Alu.mult))
            vmx = vbar()
            wexp = [("v", vmx)]
            if pt_t_by_par[i % 2]:
                wexp.append(("t", pt_t_by_par[i % 2]))
            E("scalar", lambda e, Sap=Sap, L5=L5, Pv=Pv: e.activation(
                Pv[:, 0:L5], Sap, Act.Exp, bias=nmax[:],
                scale=0.08838834764831845, accum_out=rsum[:]), wait=wexp)
            em = sbar()
            for b_ in range(r0, r0 + ngr):
                bank_guard[b_] = em
            wln_ = [("v", vsc_prev)] if vsc_prev else []
            E("scalar", lambda e: e.activation(rcp[:], rsum[:], Act.Ln), wait=wln_)
            E("scalar", lambda e: e.drain())
            E("scalar", lambda e: e.activation(rcp[:], rcp[:], Act.Exp, scale=-1.0))
            smn = sbar()
            E("vector", lambda e, L5=L5, Pv=Pv: e.tensor_scalar(
                Pv[:, 0:L5], Pv[:, 0:L5], rcp[:], None, op0=Alu.mult),
              wait=[("s", smn)])
            vsc_prev = vbar()
            vsc = ("v", vsc_prev)
            if i + 1 < ng:
                tS_ms = emit_S(i + 1)
            wPT = [vsc]
            nw = (nkt + 7) // 8
            for w_ in range(nw):
                t0 = w_ * 8
                t1 = min(nkt, t0 + 8)
                for kt in range(t0, t1):
                    E("tensor", lambda e, kt=kt, t0=t0, Pv=Pv: e.transpose(
                        bank_bf(6)[:, (kt - t0) * 128:(kt - t0) * 128 + 128],
                        Pv[:, kt * 128:kt * 128 + 128], id_sb[:]),
                      wait=wPT if kt == t0 else (),
                      inc=[("t", 1)] if kt == t1 - 1 else ())
                tw = C("t")
                if w_ == nw - 1:
                    pt_t_by_par[i % 2] = tw
                pt_ev = E("scalar", lambda e, t0=t0, t1=t1: e.copy(
                    PT_sb[:, t0:t1, :],
                    bank_bf(6)[:, 0:(t1 - t0) * 128].rearrange(
                        "p (a b) -> p a b", b=128)),
                    wait=[("t", tw)], inc=[("s", 1)])
                wPT = [("s", pt_ev)]
            wPV = [("s", pt_ev)]
            if pv_ev:
                wPV.append(("s", pv_ev))
            for kt in range(nkt):
                E("tensor", lambda e, kt=kt, kv=kv: e.matmul(
                    pvap, v_sb[:, kt, kv * 128:kv * 128 + 128], PT_sb[:, kt, :],
                    start=(kt == 0), stop=(kt == nkt - 1)),
                  wait=wPV if kt == 0 else (),
                  inc=[("t", 1)] if kt == nkt - 1 else ())
            tpv = C("t")
            pv_ev = E("scalar", lambda e, h=h, qb=qb: e.copy(
                ao_sb[:, h, qb * 128:qb * 128 + 128], pvap),
                wait=[("t", tpv)], inc=[("s", 1)])
        a3_done_s = C("s")

        vst_prev = 0
        sst_prev = 0
        y1t_last_t = 0
        y1t_ev = [0, 0]
        xres_used = [0, 0]
        pend = None

        def y1t_emit(ptc, lnv):
            nonlocal_state = None
            for bh in range(2):
                wt2 = [("v", lnv)]
                if y1t_ev[bh]:
                    wt2.append(("v", y1t_ev[bh]))
                for kcl in range(8):
                    kc = bh * 8 + kcl
                    E("tensor", lambda e, bh=bh, kcl=kcl, kc=kc: e.transpose(
                        bank_bf(4 + bh)[0:64, kcl * 128:kcl * 128 + 128],
                        P_sb[:, kc * 128:kc * 128 + 128:2], id_sb[:]),
                      wait=wt2 if kcl == 0 else ())
                    E("tensor", lambda e, bh=bh, kcl=kcl, kc=kc: e.transpose(
                        bank_bf(4 + bh)[64:128, kcl * 128:kcl * 128 + 128],
                        P_sb[:, kc * 128 + 1:kc * 128 + 128:2], id_sb[:]),
                      inc=[("t", 1)] if kcl == 7 else ())
                tm2 = C("t")
                y1t_ev[bh] = E("vector", lambda e, bh=bh, ptc=ptc: e.tensor_copy(
                    y1T[:, bh * 8:bh * 8 + 8, ptc * 128:ptc * 128 + 128],
                    bank_bf(4 + bh)[:].rearrange("p (a b) -> p a b", b=128)),
                    wait=[("t", tm2)], inc=[("v", 1)])
            return C("t")

        for tc in range(8):
            bu = tc % 2
            wxr = [("s", a3_done_s)] if tc < 2 else [("v", xres_used[bu])]
            xm = DMA("sync", lambda e, tc=tc, bu=bu: e.dma_start(
                out=xres[bu][:], in_=x_q[tc * 128:tc * 128 + 128, :]),
                wait=wxr)
            wmm = [("s", a3_done_s)] + (wo_lms if tc == 0 else [])
            if vst_prev:
                wmm.append(("v", vst_prev))
            for og in range(4):
                for kc in range(16):
                    E("tensor", lambda e, tc=tc, og=og, kc=kc: e.matmul(
                        bank(og), ao_sb[:, kc, tc * 128:tc * 128 + 128],
                        wo_sb[:, kc, og * 512:og * 512 + 512],
                        start=(kc == 0), stop=(kc == 15)),
                      wait=wmm if (og == 0 and kc == 0) else (),
                      inc=[("t", 1)] if (og == 3 and kc == 15) else ())
            tm = C("t")
            if pend is not None:
                y1t_last_t = y1t_emit(*pend)
            for og in range(4):
                wstt = [("t", tm), xm] if og == 0 else []
                if og == 0 and sst_prev:
                    wstt.append(("s", sst_prev))
                E("vector", lambda e, og=og, bu=bu: e.scalar_tensor_tensor(
                    y1pre[:, og * 512:og * 512 + 512],
                    xres[bu][:, og * 512:og * 512 + 512], 2.0, bank(og),
                    op0=Alu.mult, op1=Alu.add),
                  wait=wstt)
            E("vector", lambda e: e.drain())
            E("vector", lambda e: e.tensor_reduce(nmean[:], y1pre[:], axis=AX,
                                                  op=Alu.add, negate=True))
            E("vector", lambda e: e.drain())
            E("vector", lambda e: e.tensor_single_scalar(
                nmean[:], nmean[:], 1.0 / 2048, Alu.mult))
            vst = vbar()
            xres_used[bu] = vst
            vst_prev = vst
            E("scalar", lambda e: e.activation(sq2[:, 0:2048], y1pre[:], Act.Square,
                                               bias=nmean[:], accum_out=ssd[:]),
              wait=[("v", vst)])
            E("scalar", lambda e: e.drain())
            E("scalar", lambda e: e.activation(rstd[:], ssd[:], Act.Ln,
                                               scale=1.0 / 2048, bias=eps5[:]))
            E("scalar", lambda e: e.drain())
            E("scalar", lambda e: e.activation(rstd[:], rstd[:], Act.Exp, scale=-0.5))
            sst = sbar()
            sst_prev = sst
            wln = [("s", sst)]
            if y1t_last_t:
                wln.append(("t", y1t_last_t))
            E("vector", lambda e: e.tensor_scalar(
                P_sb[:], y1pre[:], nmean[:], rstd[:], op0=Alu.add, op1=Alu.mult),
              wait=wln)
            lnv = vbar()
            pend = (tc, lnv)
        y1t_last_t = y1t_emit(*pend)
        a4_done_v = C("v")

        gu_mm = [0, 0]
        inter_ev = [0, 0]
        dn_mm = [0, 0]
        y2_add = [0, 0]
        dn_last_t = 0
        first = True
        for sc in range(8):
            for nc2 in range(16):
                nch = sc * 16 + nc2
                par = nc2 % 2
                wld = ([("t", gu_mm[par])] if gu_mm[par]
                       else [("cc", 2), ("v", a4_done_v)])
                glm1 = DMA("sync", lambda e, nch=nch, par=par: e.dma_start(
                    out=gub[:, par * 2, :, :], in_=g_fu.ap()[nch]), wait=wld)
                glm2 = DMA("sync", lambda e, nch=nch, par=par: e.dma_start(
                    out=gub[:, par * 2 + 1, :, :], in_=u_fu.ap()[nch]))
                reg = par * 4
                wmm = [glm1, glm2]
                if inter_ev[par]:
                    wmm.append(("v", inter_ev[par]))
                if sc > 0 and nc2 < 2:
                    wmm += [("v", y2_add[0]), ("v", y2_add[1])]
                if sc == 0 and nc2 < 2:
                    wmm.append(("v", a4_done_v))
                for kc in range(16):
                    for th in range(2):
                        E("tensor", lambda e, par=par, kc=kc, th=th, reg=reg:
                          e.matmul(bank(reg + th), gub[:, par * 2, kc, :],
                                   y1T[:, kc, th * 512:th * 512 + 512],
                                   start=(kc == 0), stop=(kc == 15)),
                          wait=wmm if (kc == 0 and th == 0) else ())
                        E("tensor", lambda e, par=par, kc=kc, th=th, reg=reg:
                          e.matmul(bank(reg + 2 + th), gub[:, par * 2 + 1, kc, :],
                                   y1T[:, kc, th * 512:th * 512 + 512],
                                   start=(kc == 0), stop=(kc == 15)),
                          inc=[("t", 1)] if (kc == 15 and th == 1) else ())
                gu_mm[par] = C("t")
                wge = [("t", gu_mm[par])]
                if inter_ev[par]:
                    wge.append(("v", inter_ev[par]))
                E("scalar", lambda e, par=par, reg=reg: e.activation(
                    sq2[:, par * 1024:par * 1024 + 1024], bank(reg, 2),
                    Act.Gelu_apprx_tanh), wait=wge)
                sg = sbar()
                wiv = [("s", sg)]
                if dn_last_t:
                    wiv.append(("t", dn_last_t))
                inter_ev[par] = E("vector", lambda e, nc2=nc2, par=par, reg=reg:
                                  e.tensor_tensor(
                                      inter[:, nc2, :], bank(reg + 2, 2),
                                      sq2[:, par * 1024:par * 1024 + 1024], Alu.mult),
                                  wait=wiv, inc=[("v", 1)])
            iv = C("v")
            for hc in range(16):
                par = hc % 2
                wld = [("t", dn_mm[par])] if dn_mm[par] else [("cc", ag_done),
                                                             ("v", a4_done_v)]
                dlm = DMA("sync", lambda e, sc=sc, hc=hc, par=par: e.dma_start(
                    out=dnb[:, par, :, :], in_=d_fu.ap()[sc * 16 + hc]),
                    wait=wld)
                reg = par * 2
                wmm = [dlm, ("v", iv)]
                if y2_add[par]:
                    wmm.append(("v", y2_add[par]))
                for nc2 in range(16):
                    for th in range(2):
                        E("tensor", lambda e, par=par, nc2=nc2, th=th, reg=reg:
                          e.matmul(bank(reg + th), dnb[:, par, nc2, :],
                                   inter[:, nc2, th * 512:th * 512 + 512],
                                   start=(nc2 == 0), stop=(nc2 == 15)),
                          wait=wmm if (nc2 == 0 and th == 0) else (),
                          inc=[("t", 1)] if (nc2 == 15 and th == 1) else ())
                dn_mm[par] = C("t")
                if first and hc < 2:
                    y2_add[par] = E("vector", lambda e, hc=hc, reg=reg: e.tensor_copy(
                        y2acc[:, hc, :], bank(reg, 2)),
                        wait=[("t", dn_mm[par])], inc=[("v", 1)])
                elif first:
                    y2_add[par] = E("vector", lambda e, hc=hc, reg=reg: e.tensor_copy(
                        y2acc[:, hc, :], bank(reg, 2)),
                        wait=[("t", dn_mm[par]), ("v", y2_add[par])], inc=[("v", 1)])
                else:
                    y2_add[par] = E("vector", lambda e, hc=hc, reg=reg:
                                    e.scalar_tensor_tensor(
                                        y2acc[:, hc, :], bank(reg, 2), 1.0,
                                        y2acc[:, hc, :], op0=Alu.mult, op1=Alu.add),
                                    wait=[("t", dn_mm[par]), ("v", y2_add[par])],
                                    inc=[("v", 1)])
            dn_last_t = C("t")
            first = False
        ffn_done_v = C("v")

        for kc in range(16):
            E("vector", lambda e, kc=kc: e.tensor_tensor(
                y2acc[:, kc, :], y2acc[:, kc, :], y1T[:, kc, :], Alu.add),
              wait=[("v", ffn_done_v)] if kc == 0 else ())
        E("vector", lambda e: e.drain())
        for kc in range(16):
            E("vector", lambda e, kc=kc: e.tensor_copy(y1T[:, kc, :], y2acc[:, kc, :]))
        vy2b = vbar()
        E("scalar", lambda e: e.activation(ar2[:], ar3[:], Act.Square),
          wait=[("v", vy2b)])
        ssq = sbar()
        st_ms = [0, 0]
        vvar_prev = 0
        for tb in range(2):
            w0 = [("s", ssq)] if tb == 0 else [("v", vvar_prev)]
            for kc in range(16):
                E("tensor", lambda e, tb=tb, kc=kc: e.matmul(
                    bank(0)[0:1, 0:512], ones_b[:],
                    y1T[:, kc, tb * 512:tb * 512 + 512],
                    start=(kc == 0), stop=(kc == 15)),
                  wait=w0 if kc == 0 else ())
            for kc in range(16):
                E("tensor", lambda e, tb=tb, kc=kc: e.matmul(
                    bank(1)[0:1, 0:512], ones_b[:],
                    inter[:, kc, tb * 512:tb * 512 + 512],
                    start=(kc == 0), stop=(kc == 15)),
                  inc=[("t", 1)] if kc == 15 else ())
            tsum = C("t")
            E("vector", lambda e: e.tensor_single_scalar(
                ln2a[:], bank(0)[0:1, 0:512], -1.0 / 2048, Alu.mult),
              wait=[("t", tsum)])
            E("vector", lambda e: e.tensor_single_scalar(
                ln2b[:], bank(1)[0:1, 0:512], 1.0 / 2048, Alu.mult))
            E("vector", lambda e: e.drain())
            E("vector", lambda e: e.tensor_tensor(ln2c[:], ln2a[:], ln2a[:], Alu.mult))
            E("vector", lambda e: e.drain())
            E("vector", lambda e: e.tensor_tensor(ln2b[:], ln2b[:], ln2c[:],
                                                  Alu.subtract))
            vvar_prev = vvar = vbar()
            E("scalar", lambda e: e.activation(ln2c[:], ln2b[:], Act.Ln,
                                               bias=eps5[0:1, :]),
              wait=[("v", vvar)])
            E("scalar", lambda e: e.drain())
            E("scalar", lambda e: e.activation(ln2c[:], ln2c[:], Act.Exp, scale=-0.5))
            srs = sbar()
            wbc = [("s", srs)]
            if st_ms[0]:
                wbc.append(("v", st_ms[0]))
            E("tensor", lambda e: e.matmul(bank(2), ones_f[:], ln2a[:],
                                           start=True, stop=True), wait=wbc)
            E("tensor", lambda e: e.matmul(bank(3), ones_f[:], ln2c[:],
                                           start=True, stop=True), inc=[("t", 1)])
            tbc = C("t")
            for kc in range(16):
                E("vector", lambda e, tb=tb, kc=kc: e.tensor_tensor(
                    y2acc[:, kc, tb * 512:tb * 512 + 512],
                    y2acc[:, kc, tb * 512:tb * 512 + 512], bank(2), Alu.add),
                  wait=[("t", tbc)] if kc == 0 else ())
            E("vector", lambda e: e.drain())
            for kc in range(16):
                E("vector", lambda e, tb=tb, kc=kc: e.tensor_tensor(
                    y2acc[:, kc, tb * 512:tb * 512 + 512],
                    y2acc[:, kc, tb * 512:tb * 512 + 512], bank(3), Alu.mult))
            st_ms[tb] = vbar()
        ev_fin = 0
        st_fin = [0, 0]
        for tc in range(8):
            bu = tc % 2
            wt = [("v", st_ms[tc // 4])]
            if ev_fin:
                wt.append(("v", ev_fin))
            for kc in range(16):
                bk = 4 + kc // 4
                E("tensor", lambda e, tc=tc, kc=kc, bk=bk: e.transpose(
                    bank(bk)[:, (kc % 4) * 128:(kc % 4) * 128 + 128],
                    y2acc[:, kc, tc * 128:tc * 128 + 128], idf_sb[:]),
                  wait=wt if kc == 0 else (),
                  inc=[("t", 1)] if kc == 15 else ())
            tfin = C("t")
            wev = [("t", tfin)]
            if st_fin[bu]:
                wev.append(st_fin[bu])
            for bk in range(4):
                E("vector", lambda e, bu=bu, bk=bk: e.tensor_copy(
                    y2tok[bu][:, bk * 512:bk * 512 + 512].rearrange(
                        "p (a j two) -> p a two j", a=4, two=2),
                    bank(4 + bk)[:].rearrange("p (a b) -> p a b", b=128).rearrange(
                        "p a (two j) -> p a two j", two=2)),
                  wait=wev if bk == 0 else ())
            ev_fin = vbar()
            st_fin[bu] = DMA("gpsimd", lambda e, tc=tc, bu=bu: e.dma_start(
                out=y_h[tc * 128:tc * 128 + 128, :], in_=y2tok[bu][:]),
                wait=[("v", ev_fin)])
        E("gpsimd", lambda e: e.nop(),
          wait=(bld.dma_totals("gpsimd") + bld.dma_totals("sync")
                + bld.dma_totals("scalar")))

        for ename in ENGS:
            entries = bld.entries[ename]

            def runner(eng, _entries=entries):
                for wait, f, incs in _entries:
                    for s_, v_ in wait:
                        eng.wait_ge(SEM[s_], v_)
                    ins = f(eng)
                    for s_, n_ in incs:
                        ins.then_inc(SEM[s_], n_)
            getattr(blk, ename)(runner)

    n_inst = sum(len(v) for v in bld.entries.values())
    print(f"[kernel] emitted {n_inst} entries "
          + str({k: len(v) for k, v in bld.entries.items()}))
    return nc


def kernel(**inputs):
    x = np.asarray(inputs["L0_attn_x"], dtype=np.float32)
    wq = np.asarray(inputs["L0_attn_Wq"], dtype=np.float32)
    wk = np.asarray(inputs["L0_attn_Wk"], dtype=np.float32)
    wv = np.asarray(inputs["L0_attn_Wv"], dtype=np.float32)
    wo = np.asarray(inputs["L0_attn_Wo"], dtype=np.float32)
    qg = np.ascontiguousarray(
        np.asarray(inputs["L0_attn_q_gamma"], dtype=np.float32).reshape(128, 1))
    kg = np.ascontiguousarray(
        np.asarray(inputs["L0_attn_k_gamma"], dtype=np.float32).reshape(128, 1))
    gp = np.asarray(inputs["L0_ffn_gate_W_packed"], dtype=np.int32)
    gs = np.asarray(inputs["L0_ffn_gate_scales"], dtype=np.float32)
    up = np.asarray(inputs["L0_ffn_up_W_packed"], dtype=np.int32)
    us = np.asarray(inputs["L0_ffn_up_scales"], dtype=np.float32)
    dp = np.asarray(inputs["L0_ffn_down_W_packed"], dtype=np.int32)
    ds = np.asarray(inputs["L0_ffn_down_scales"], dtype=np.float32)

    ident = np.eye(128, dtype=ml_dtypes.bfloat16)
    identf = np.eye(128, dtype=np.float32)

    in_maps = []
    for c in range(NCORES):
        b, par = c // 2, c % 2
        blocks = QBLOCKS[par]
        xq_rows = np.concatenate([x[b, g * 128:(g + 1) * 128, :] for g in blocks], 0)
        mk = np.zeros((8, 128, 512), dtype=np.float32)
        for qb, g in enumerate(blocks):
            L5 = ((MPAT[qb] + 1 + 3) // 4) * 512
            k0 = L5 - 512
            kk = k0 + np.arange(512)[None, :]
            qq = g * 128 + np.arange(128)[:, None]
            mk[qb] = np.where(kk <= qq, 0.0, NEG)
        in_maps.append({
            "x_full": np.ascontiguousarray(x[b]),
            "x_q": np.ascontiguousarray(xq_rows),
            "wq": wq, "wk": wk, "wv": wv, "wo": wo,
            "gp": np.ascontiguousarray(gp[c * 2048:(c + 1) * 2048].astype(np.int16)),
            "gs": np.ascontiguousarray(gs[c * 2048:(c + 1) * 2048]),
            "up": np.ascontiguousarray(up[c * 2048:(c + 1) * 2048].astype(np.int16)),
            "us": np.ascontiguousarray(us[c * 2048:(c + 1) * 2048]),
            "dp": np.ascontiguousarray(dp[:, c * 1024:(c + 1) * 1024].astype(np.int16)),
            "ds": np.ascontiguousarray(ds[:, c * 64:(c + 1) * 64]),
            "qg": qg, "kg": kg, "ident": ident, "identf": identf, "maskqb": mk,
        })

    nc = build_nc()
    res = run_bass_kernel_spmd(nc, in_maps, core_ids=list(range(NCORES)))

    out = np.empty((B, S, D), dtype=np.float32)
    for c in range(NCORES):
        b, par = c // 2, c % 2
        y = res.results[c]["y"]
        for qb, g in enumerate(QBLOCKS[par]):
            out[b, g * 128:(g + 1) * 128, :] = y[qb * 128:(qb + 1) * 128, :]
    return out


# version 23 (final)
# speedup vs baseline: 1.0706x; 1.0003x over previous
import contextlib
import numpy as np
import ml_dtypes
import concourse.bass as bass
import concourse.mybir as mybir
from concourse.bass_utils import run_bass_kernel_spmd

F32 = mybir.dt.float32
BF16 = mybir.dt.bfloat16
I32 = mybir.dt.int32
I16 = mybir.dt.int16
Alu = mybir.AluOpType
Act = mybir.ActivationFunctionType
AX = mybir.AxisListType.X

B, S, D = 4, 2048, 2048
H, KVH, DH = 16, 4, 128
NCORES = 8
NEG = -1e30
TQ = 1024

QBLOCKS = [[0, 3, 4, 7, 8, 11, 12, 15], [1, 2, 5, 6, 9, 10, 13, 14]]
MPAT = [1, 3, 5, 7, 9, 11, 13, 15]

ENGS = ("sync", "tensor", "vector", "scalar", "gpsimd")


NSLOT = 8


class Build:

    def __init__(self):
        self.entries = {e: [] for e in ENGS}
        self.cnt = {}
        self.slot_i = {"sync": 0, "gpsimd": 0, "scalar": 0}

    def emit(self, eng, f, wait=(), inc=()):
        self.entries[eng].append((tuple(wait), f, tuple(inc)))
        out = None
        for s, n in inc:
            self.cnt[s] = self.cnt.get(s, 0) + n
            out = self.cnt[s]
        return out

    QPFX = {"sync": "ld", "gpsimd": "gd", "scalar": "sd"}

    def dma(self, eng, f, wait=()):
        q = self.QPFX[eng]
        i = self.slot_i[eng] % NSLOT
        self.slot_i[eng] += 1
        s = f"{q}{i}"
        uses = self.cnt.get(s, 0)
        w = list(wait)
        if uses:
            w.append((s, uses))
        v = self.emit(eng, f, wait=w, inc=[(s, 16)])
        return (s, v)

    def dma_totals(self, eng):
        q = self.QPFX[eng]
        return [(f"{q}{i}", self.cnt.get(f"{q}{i}", 0)) for i in range(NSLOT)
                if self.cnt.get(f"{q}{i}", 0)]

    def cur(self, s):
        return self.cnt.get(s, 0)


def build_nc():
    nc = bass.Bass("TRN2", target_bir_lowering=False, debug=False)
    KJ, NB = 1024, 64

    x_full = nc.dram_tensor("x_full", [S, D], F32, kind="ExternalInput").ap()
    x_q = nc.dram_tensor("x_q", [TQ, D], F32, kind="ExternalInput").ap()
    wq_h = nc.dram_tensor("wq", [D, D], F32, kind="ExternalInput").ap()
    wk_h = nc.dram_tensor("wk", [D, KVH * DH], F32, kind="ExternalInput").ap()
    wv_h = nc.dram_tensor("wv", [D, KVH * DH], F32, kind="ExternalInput").ap()
    wo_h = nc.dram_tensor("wo", [D, D], F32, kind="ExternalInput").ap()
    gp_h = nc.dram_tensor("gp", [2048, KJ], I16, kind="ExternalInput").ap()
    gs_h = nc.dram_tensor("gs", [2048, NB], F32, kind="ExternalInput").ap()
    up_h = nc.dram_tensor("up", [2048, KJ], I16, kind="ExternalInput").ap()
    us_h = nc.dram_tensor("us", [2048, NB], F32, kind="ExternalInput").ap()
    dp_h = nc.dram_tensor("dp", [2048, KJ], I16, kind="ExternalInput").ap()
    ds_h = nc.dram_tensor("ds", [2048, NB], F32, kind="ExternalInput").ap()
    qg_h = nc.dram_tensor("qg", [128, 1], F32, kind="ExternalInput").ap()
    kg_h = nc.dram_tensor("kg", [128, 1], F32, kind="ExternalInput").ap()
    id_h = nc.dram_tensor("ident", [128, 128], BF16, kind="ExternalInput").ap()
    idf_h = nc.dram_tensor("identf", [128, 128], F32, kind="ExternalInput").ap()
    mk_h = nc.dram_tensor("maskqb", [8, 128, 512], F32, kind="ExternalInput").ap()
    y_h = nc.dram_tensor("y", [TQ, D], F32, kind="ExternalOutput").ap()

    wqb_h = nc.dram_tensor("wq_bf", [D, D], BF16).ap()
    wkb_h = nc.dram_tensor("wk_bf", [D, KVH * DH], BF16).ap()
    wvb_h = nc.dram_tensor("wv_bf", [D, KVH * DH], BF16).ap()
    wob_h = nc.dram_tensor("wo_bf", [D, D], BF16).ap()
    g_sh = nc.dram_tensor("g_sh", [16, 128, 16, 128], BF16)
    u_sh = nc.dram_tensor("u_sh", [16, 128, 16, 128], BF16)
    d_sh = nc.dram_tensor("d_sh", [16, 128, 16, 128], BF16)
    g_fu = nc.dram_tensor("g_fu", [128, 128, 16, 128], BF16)
    u_fu = nc.dram_tensor("u_fu", [128, 128, 16, 128], BF16)
    d_fu = nc.dram_tensor("d_fu", [128, 128, 16, 128], BF16)

    bld = Build()
    E, C = bld.emit, bld.cur

    def vbar():
        return E("vector", lambda e: e.drain(), inc=[("v", 1)])

    def sbar():
        return E("scalar", lambda e: e.drain(), inc=[("s", 1)])

    with contextlib.ExitStack() as ST:
        alloc = ST.enter_context

        ar1 = alloc(nc.sbuf_tensor("ar1", [128, 32768], BF16))
        ar2 = alloc(nc.sbuf_tensor("ar2", [128, 16384], BF16))
        ar3 = alloc(nc.sbuf_tensor("ar3", [128, 16384], BF16))
        ar4 = alloc(nc.sbuf_tensor("ar4", [128, 8192], BF16))
        ar5 = alloc(nc.sbuf_tensor("ar5", [128, 8192], BF16))
        ar6 = alloc(nc.sbuf_tensor("ar6", [128, 8192], BF16))
        ar7 = alloc(nc.sbuf_tensor("ar7", [128, 8192], BF16))
        arp = alloc(nc.sbuf_tensor("arp", [128, 4096], BF16))
        id_sb = alloc(nc.sbuf_tensor("id_sb", [128, 128], BF16))
        idf_sb = alloc(nc.sbuf_tensor("idf_sb", [128, 128], F32))
        qg_sb = alloc(nc.sbuf_tensor("qg_sb", [128, 1], F32))
        kg_sb = alloc(nc.sbuf_tensor("kg_sb", [128, 1], F32))
        ones_b = alloc(nc.sbuf_tensor("ones_b", [128, 1], BF16))
        ones_f = alloc(nc.sbuf_tensor("ones_f", [1, 128], F32))
        nmax = alloc(nc.sbuf_tensor("nmax", [128, 1], F32))
        rsum = alloc(nc.sbuf_tensor("rsum", [128, 1], F32))
        rcp = alloc(nc.sbuf_tensor("rcp", [128, 1], F32))
        nmean = alloc(nc.sbuf_tensor("nmean", [128, 1], F32))
        ssd = alloc(nc.sbuf_tensor("ssd", [128, 1], F32))
        rstd = alloc(nc.sbuf_tensor("rstd", [128, 1], F32))
        sq2 = alloc(nc.sbuf_tensor("sq2", [128, 2048], BF16))
        eps8 = alloc(nc.sbuf_tensor("eps8", [128, 1], F32))
        eps5 = alloc(nc.sbuf_tensor("eps5", [128, 1], F32))
        ps = alloc(nc.psum_tensor("ps", [128, 4096], F32))

        SEM = {}
        semnames = ["t", "v", "s", "cc", "wc"]
        semnames += [f"ld{i}" for i in range(NSLOT)] + [f"gd{i}" for i in range(NSLOT)]
        semnames += [f"sd{i}" for i in range(NSLOT)]
        for s_ in semnames:
            SEM[s_] = alloc(nc.semaphore(s_))
        blk = alloc(nc.Block())

        def bank(i, n=1):
            return ps[:, 512 * i:512 * (i + n)]

        def bank_bf(i, n=1):
            return ps[:, 512 * i:512 * (i + n)].bitcast(BF16)

        xfT = ar1[:].rearrange("p (a b) -> p a b", b=2048)
        wo_sb = ar1[:].rearrange("p (a b) -> p a b", b=2048)
        y2acc = ar1[:].bitcast(F32).rearrange("p (a b) -> p a b", b=1024)
        xqT = ar2[:].rearrange("p (a b) -> p a b", b=1024)
        wv_sb = ar2[:, 8192:16384].rearrange("p (a b) -> p a b", b=512)
        ao_sb = ar2[:].rearrange("p (a b) -> p a b", b=1024)
        inter = ar2[:].rearrange("p (a b) -> p a b", b=1024)
        qhat = ar3[:].rearrange("p (a b) -> p a b", b=1024)
        y1T = ar3[:].rearrange("p (a b) -> p a b", b=1024)
        khat = ar4[:].rearrange("p (a b) -> p a b", b=2048)
        gub = ar4[:].rearrange("p (a b c) -> p a b c", b=16, c=128)
        v_sb = ar5[:].rearrange("p (a b) -> p a b", b=512)
        dnb = ar5[:, 0:4096].rearrange("p (a b c) -> p a b c", b=16, c=128)
        y1pre = ar5[:, 0:4096].bitcast(F32)
        dq_b32 = [ar3[:, i * 1024:(i + 1) * 1024].bitcast(I16) for i in range(6)]
        dq_m = arp[:, 0:1024].bitcast(I16)
        dq_m2 = arp[:, 1024:2048].bitcast(I16)
        dq_bb = arp[:, 2048:3072].bitcast(I16)
        dq_bbh = arp[:, 3072:4096].bitcast(I16)
        dq_s2 = sq2[:, 0:1024].bitcast(I16)
        dq_s2h = sq2[:, 1024:2048].bitcast(I16)
        dq_wraw = [ar5[:, 5120:7168], ar4[:, 6144:8192]]
        dq_bits = [w.bitcast(I16).rearrange("p (a b) -> p a b", b=1024) for w in dq_wraw]
        dq_w = [w.rearrange("p (a b) -> p a b", b=1024) for w in dq_wraw]
        dq_sc = [ar5[:, i * 128:(i + 1) * 128].bitcast(F32) for i in range(6)]
        dq_st = [ar5[:, 1024:3072].rearrange("p (a b) -> p a b", b=128),
                 ar5[:, 3072:5120].rearrange("p (a b) -> p a b", b=128)]
        masks = ar7[:].bitcast(F32).rearrange("p (a b) -> p a b", b=512)
        pcs_f = [ar4[:, i * 2048:(i + 1) * 2048].bitcast(F32) for i in range(3)]
        pcs_b = [ar3[:, 14336:15360], ar3[:, 15360:16384], ar5[:, 7168:8192]]
        xf32 = [ar6[:, i * 2048:(i + 1) * 2048].bitcast(F32) for i in range(4)]
        xres = [ar6[:, 0:4096].bitcast(F32), ar6[:, 4096:8192].bitcast(F32)]
        y2tok = [ar6[:, 0:4096].bitcast(F32), ar6[:, 4096:8192].bitcast(F32)]
        ln2a = arp[:, 0:1024].bitcast(F32)[0:1, 0:512]
        ln2b = arp[:, 1024:2048].bitcast(F32)[0:1, 0:512]
        ln2c = arp[:, 2048:3072].bitcast(F32)[0:1, 0:512]
        P_sb = arp[:, 0:2048]
        PT_sb = arp[:, 2048:4096].rearrange("p (a b) -> p a b", b=128)

        DMA = bld.dma
        id_ms = DMA("sync", lambda e: e.dma_start(out=id_sb[:], in_=id_h))
        idf_ms = DMA("sync", lambda e: e.dma_start(out=idf_sb[:], in_=idf_h))
        qg_ms = DMA("sync", lambda e: e.dma_start(out=qg_sb[:], in_=qg_h))
        kg_ms = DMA("sync", lambda e: e.dma_start(out=kg_sb[:], in_=kg_h))
        for qb in range(8):
            DMA("sync", lambda e, qb=qb: e.dma_start(out=masks[:, qb, :], in_=mk_h[qb]))
        const_ms = bld.dma_totals("sync")
        E("vector", lambda e: e.memset(ones_b[:], 1.0))
        E("vector", lambda e: e.memset(ones_f[:], 1.0))
        E("vector", lambda e: e.memset(eps8[:], 1e-8))
        E("vector", lambda e: e.memset(eps5[:], 1e-5))
        E("vector", lambda e: e.drain())

        pc_items = []
        for kc in range(16):
            for hf in range(2):
                pc_items.append((wq_h[kc * 128:kc * 128 + 128,
                                      hf * 1024:hf * 1024 + 1024],
                                 wqb_h[kc * 128:kc * 128 + 128,
                                       hf * 1024:hf * 1024 + 1024], 1024))
        for kc in range(16):
            pc_items.append((wk_h[kc * 128:kc * 128 + 128, :],
                             wkb_h[kc * 128:kc * 128 + 128, :], 512))
            pc_items.append((wv_h[kc * 128:kc * 128 + 128, :],
                             wvb_h[kc * 128:kc * 128 + 128, :], 512))
        for kc in range(16):
            for hf in range(2):
                pc_items.append((wo_h[kc * 128:kc * 128 + 128,
                                      hf * 1024:hf * 1024 + 1024],
                                 wob_h[kc * 128:kc * 128 + 128,
                                       hf * 1024:hf * 1024 + 1024], 1024))
        pc_lm = {}
        pc_sm = [0, 0, 0]

        def pc_load(it):
            if it >= len(pc_items):
                return
            src_, dst_, w_ = pc_items[it]
            sl = it % 3
            wld = [pc_sm[sl]] if pc_sm[sl] else []
            pc_lm[it] = DMA("gpsimd", lambda e, src_=src_, sl=sl, w_=w_:
                            e.dma_start(out=pcs_f[sl][:, 0:w_], in_=src_),
                            wait=wld)

        def pc_proc(i0):
            for it in (i0, i0 + 1):
                sl = it % 3
                w_ = pc_items[it][2]
                E("scalar", lambda e, sl=sl, w_=w_: e.activation(
                    pcs_b[sl][:, 0:w_], pcs_f[sl][:, 0:w_], Act.Identity),
                  wait=[pc_lm[it]])
            pcst = sbar()
            for it in (i0, i0 + 1):
                sl = it % 3
                dst_, w_ = pc_items[it][1], pc_items[it][2]
                pc_sm[sl] = DMA("sync", lambda e, dst_=dst_, sl=sl, w_=w_:
                                e.dma_start(out=dst_, in_=pcs_b[sl][:, 0:w_]),
                                wait=[("s", pcst)])

        def dq_chain(v, b6, t2):
            b16v = dq_b32[b6]
            v.tensor_single_scalar(dq_m[:], b16v[:], 7, Alu.bitwise_and)
            v.tensor_single_scalar(dq_m2[:], b16v[:], 0x70, Alu.bitwise_and)
            v.tensor_single_scalar(dq_s2[:], b16v[:], 8, Alu.bitwise_and)
            v.tensor_single_scalar(dq_s2h[:], b16v[:], 128, Alu.bitwise_and)
            v.tensor_scalar(dq_bb[:], dq_m[:], 2, 16128, op0=Alu.min, op1=Alu.mult)
            v.tensor_scalar(dq_bbh[:], dq_m2[:], 32, 1008, op0=Alu.min, op1=Alu.mult)
            v.tensor_single_scalar(dq_m[:], dq_m[:], 64, Alu.mult)
            v.tensor_single_scalar(dq_m2[:], dq_m2[:], 4, Alu.mult)
            v.scalar_tensor_tensor(dq_m[:], dq_m[:], 16128.0, dq_bb[:],
                                   op0=Alu.add, op1=Alu.min)
            v.scalar_tensor_tensor(dq_m2[:], dq_m2[:], 16128.0, dq_bbh[:],
                                   op0=Alu.add, op1=Alu.min)
            v.scalar_tensor_tensor(dq_bits[t2][:, 0, :], dq_s2[:], -4096.0,
                                   dq_m[:], op0=Alu.mult, op1=Alu.add)
            v.scalar_tensor_tensor(dq_bits[t2][:, 1, :], dq_s2h[:], -256.0,
                                   dq_m[:] if False else dq_m2[:],
                                   op0=Alu.mult, op1=Alu.add)
            sc_b = dq_sc[b6][:].unsqueeze(2).broadcast_to([128, 64, 16])
            for pl in range(2):
                bits_bf = dq_bits[t2][:, pl, :].bitcast(BF16).rearrange(
                    "n (b u) -> n b u", u=16)
                v.tensor_tensor(dq_w[t2][:, pl, :].rearrange(
                    "n (b u) -> n b u", u=16), bits_bf, sc_b, Alu.mult)
            return v.drain()

        mats = ((gp_h, gs_h, g_sh), (up_h, us_h, u_sh), (dp_h, ds_h, d_sh))
        a1_items = ([(x_full, xfT, tt, hf) for tt in range(16) for hf in range(2)]
                    + [(x_q, xqT, tt, hf) for tt in range(8) for hf in range(2)])
        a1_lm = {}
        a1_tr = [0, 0, 0, 0]
        a1_ev = [0, 0]
        a1_g = 0

        def a1_load(hi):
            if hi >= len(a1_items):
                return
            xd, dst, tt, hf = a1_items[hi]
            sl = hi % 4
            wt0 = [("t", a1_tr[sl])] if a1_tr[sl] else []
            a1_lm[hi] = DMA("sync", lambda e, xd=xd, tt=tt, hf=hf, sl=sl:
                            e.dma_start(out=xf32[sl][:],
                                        in_=xd[tt * 128:tt * 128 + 128,
                                               hf * 1024:hf * 1024 + 1024]),
                            wait=wt0)

        dq_cons = [0] * 6

        def dq_load(g2):
            if g2 >= 48:
                return (None, None)
            mi2, oc2 = g2 // 16, g2 % 16
            pk2, scl2 = mats[mi2][0], mats[mi2][1]
            b62 = g2 % 6
            wld = []
            mm1 = DMA("scalar", lambda e, pk2=pk2, oc2=oc2, b62=b62: e.dma_start(
                out=dq_b32[b62][:],
                in_=pk2[oc2 * 128:oc2 * 128 + 128, :].rearrange(
                    "(j two) c -> two j c", two=2)), wait=wld)
            mm2 = DMA("scalar", lambda e, scl2=scl2, oc2=oc2, b62=b62: e.dma_start(
                out=dq_sc[b62][:],
                in_=scl2[oc2 * 128:oc2 * 128 + 128, :].rearrange(
                    "(j two) c -> two j c", two=2)))
            return (mm1, mm2)

        dq_lms = {}
        dq_tr = [0, 0]
        dq_evac = [0, 0]
        dq_store = [0, 0]
        dq_vm = 0
        a1_load(0)
        a1_load(1)
        a1_load(2)
        pc_load(0)
        pc_load(1)
        for g2 in range(6):
            dq_lms[g2] = dq_load(g2)
        for goc in range(48):
            mi, oc = goc // 16, goc % 16
            sh = mats[mi][2]
            t2 = goc % 2
            m1, m2 = dq_lms[goc]
            wv_ = [m1, m2]
            if dq_tr[t2]:
                wv_.append(("t", dq_tr[t2]))
            dq_vm = E("vector", lambda e, b6=goc % 6, t2=t2: dq_chain(e, b6, t2),
                      wait=wv_, inc=[("v", 1)])
            dq_cons[goc % 6] = dq_vm
            for bh in range(2):
                wt = [("v", dq_vm)]
                if dq_evac[bh]:
                    wt.append(("s", dq_evac[bh]))
                for kcl in range(8):
                    kc = bh * 8 + kcl
                    E("tensor", lambda e, bh=bh, kcl=kcl, kc=kc, t2=t2:
                      e.transpose(
                          bank_bf(6 + bh)[0:64, kcl * 128:kcl * 128 + 128],
                          dq_w[t2][:, 0, kc * 64:kc * 64 + 64], id_sb[:]),
                      wait=wt if kcl == 0 else ())
                    E("tensor", lambda e, bh=bh, kcl=kcl, kc=kc, t2=t2:
                      e.transpose(
                          bank_bf(6 + bh)[64:128, kcl * 128:kcl * 128 + 128],
                          dq_w[t2][:, 1, kc * 64:kc * 64 + 64], id_sb[:]),
                      inc=[("t", 1)] if kcl == 7 else ())
                tm = C("t")
                wev = [("t", tm)]
                if dq_store[t2]:
                    wev.append(dq_store[t2])
                dq_evac[bh] = E("scalar", lambda e, bh=bh, t2=t2: e.copy(
                    dq_st[t2][:, bh * 8:bh * 8 + 8, :],
                    bank_bf(6 + bh)[:].rearrange("p (a b) -> p a b", b=128)),
                    wait=wev, inc=[("s", 1)])
            dq_tr[t2] = C("t")
            dq_lms[goc + 6] = dq_load(goc + 6)
            dq_store[t2] = DMA("scalar", lambda e, sh=sh, oc=oc, t2=t2:
                               e.dma_start(out=sh.ap()[oc], in_=dq_st[t2][:]),
                               wait=[("s", dq_evac[1])])
            xd, dst, tt, hf = a1_items[goc]
            sl = goc % 4
            for sub in range(2):
                bk = 4 + (a1_g % 2)
                wt2 = [a1_lm[goc]] if sub == 0 else []
                if a1_ev[a1_g % 2]:
                    wt2.append(("s", a1_ev[a1_g % 2]))
                for q in range(4):
                    kcl = sub * 4 + q
                    E("tensor", lambda e, sl=sl, kcl=kcl, bk=bk, q=q:
                      e.transpose(
                          ps[:, bk * 512 + q * 128:bk * 512 + q * 128 + 128],
                          xf32[sl][:, kcl * 128:kcl * 128 + 128], idf_sb[:]),
                      wait=wt2 if q == 0 else (),
                      inc=[("t", 1)] if q == 3 else ())
                tm = C("t")
                if sub == 1:
                    a1_tr[sl] = tm
                a1_ev[a1_g % 2] = E(
                    "scalar",
                    lambda e, dst=dst, tt=tt, hf=hf, sub=sub, bk=bk: e.copy(
                        dst[:, hf * 8 + sub * 4:hf * 8 + sub * 4 + 4,
                            tt * 128:tt * 128 + 128],
                        ps[:, bk * 512:bk * 512 + 512].rearrange(
                            "p (a b) -> p a b", b=128)),
                    wait=[("t", tm)], inc=[("s", 1)])
                a1_g += 1
            a1_load(goc + 2)
            pc_proc(2 * goc)
            pc_load(2 * goc + 2)
            pc_load(2 * goc + 3)
        xT_done_v = ("s", C("s"))
        dq_vm_last = dq_vm
        t_ph0 = C("t")
        pc_fin = bld.dma_totals("sync") + bld.dma_totals("scalar")
        dq_store_last = [dq_store[0], dq_store[1]]
        ag_w = (bld.dma_totals("scalar") + bld.dma_totals("sync")
                + bld.dma_totals("gpsimd"))

        rms_ln = arp[:].bitcast(F32)[0:1, 0:2048]

        def rms_apply(raw_ap, out_aps, nblk, reg, gam, extra=(), bank_guard=0):
            sqv = sq2[:, 0:nblk * 512]
            E("scalar", lambda e: e.activation(sqv, raw_ap, Act.Square), wait=extra)
            sm = sbar()
            for b_ in range(nblk):
                w_ss = [("s", sm)] + ([("v", bank_guard)] if bank_guard else [])
                E("tensor", lambda e, b_=b_: e.matmul(
                    ps[0:1, (reg + b_) * 512:(reg + b_) * 512 + 512], ones_b[:],
                    sqv[:, b_ * 512:b_ * 512 + 512], start=True, stop=True),
                  wait=w_ss if b_ == 0 else (),
                  inc=[("t", 1)] if b_ == nblk - 1 else ())
            tm = C("t")
            E("scalar", lambda e: e.activation(
                rms_ln[:, 0:nblk * 512], ps[0:1, reg * 512:(reg + nblk) * 512],
                Act.Ln, scale=1.0 / 128, bias=eps8[0:1, :]), wait=[("t", tm)])
            E("scalar", lambda e: e.drain())
            E("scalar", lambda e: e.activation(rms_ln[:, 0:nblk * 512],
                                               rms_ln[:, 0:nblk * 512], Act.Exp,
                                               scale=-0.5))
            sm2 = sbar()
            for b_ in range(nblk):
                E("tensor", lambda e, b_=b_: e.matmul(
                    ps[:, (reg + b_) * 512:(reg + b_) * 512 + 512], ones_f[:],
                    rms_ln[:, b_ * 512:b_ * 512 + 512], start=True, stop=True),
                  wait=[("s", sm2)] if b_ == 0 else (),
                  inc=[("t", 1)] if b_ == nblk - 1 else ())
            tm2 = C("t")
            for b_ in range(nblk):
                E("vector", lambda e, b_=b_, gam=gam: e.scalar_tensor_tensor(
                    out_aps[b_], ps[:, (reg + b_) * 512:(reg + b_) * 512 + 512],
                    gam[:], raw_ap[:, b_ * 512:b_ * 512 + 512],
                    op0=Alu.mult, op1=Alu.mult),
                  wait=[("t", tm2)] if b_ == 0 else ())
            return vbar()

        wslab = [ar4[:, i * 1024:(i + 1) * 1024] for i in range(8)]
        slab_mm = [0] * 8
        bank_ev = [0] * 8
        qev = 0
        si = 0
        for thh in range(4):
            hh8, th = thh // 2, thh % 2
            for kc in range(16):
                s8 = si % 8
                wld = [("t", slab_mm[s8])] if slab_mm[s8] else (
                    list(pc_fin) + [("t", t_ph0)] if si == 0 else [])
                lm = DMA("sync", lambda e, hh8=hh8, kc=kc, s8=s8: e.dma_start(
                    out=wslab[s8][:],
                    in_=wqb_h[kc * 128:kc * 128 + 128,
                              hh8 * 1024:hh8 * 1024 + 1024]), wait=wld)
                wmm = [lm]
                if thh == 0 and kc == 0:
                    wmm += [xT_done_v, ("v", dq_vm_last)]
                for h8 in range(8):
                    wmh = list(wmm) if h8 == 0 else []
                    if kc == 0 and bank_ev[h8]:
                        wmh.append(("s", bank_ev[h8]))
                    E("tensor", lambda e, h8=h8, kc=kc, th=th, s8=s8: e.matmul(
                        bank(h8), wslab[s8][:, h8 * 128:h8 * 128 + 128],
                        xqT[:, kc, th * 512:th * 512 + 512],
                        start=(kc == 0), stop=(kc == 15)),
                      wait=wmh,
                      inc=[("t", 1)] if h8 == 7 else ())
                slab_mm[s8] = C("t")
                si += 1
            tq_ = C("t")
            for h8 in range(8):
                wq_ev = [("t", tq_)] if h8 == 0 else []
                if h8 == 0 and thh == 0:
                    wq_ev.append(("v", dq_vm_last))
                if h8 == 0 and thh == 2:
                    wq_ev += pc_fin
                bank_ev[h8] = E("scalar", lambda e, hh8=hh8, th=th, h8=h8: e.copy(
                    qhat[:, hh8 * 8 + h8, th * 512:th * 512 + 512], bank(h8)),
                  wait=wq_ev, inc=[("s", 1)])
            qev = C("s")
        q_all_t = C("t")
        wk_sb = ar6[:].rearrange("p (a b) -> p a b", b=512)
        wk_lm = DMA("sync", lambda e: e.dma_start(
            out=wk_sb[:], in_=wkb_h.rearrange("(c p) n -> p c n", p=128)),
            wait=[("t", t_ph0)])
        for th in range(2):
            for kc in range(16):
                for kv in range(4):
                    for t2 in range(2):
                        wmh = [wk_lm] if (kc == 0 and kv == 0 and t2 == 0) else []
                        if kc == 0 and bank_ev[kv * 2 + t2]:
                            wmh.append(("s", bank_ev[kv * 2 + t2]))
                        E("tensor", lambda e, kv=kv, t2=t2, kc=kc, th=th:
                          e.matmul(bank(kv * 2 + t2),
                                   wk_sb[:, kc, kv * 128:kv * 128 + 128],
                                   xfT[:, kc, th * 1024 + t2 * 512:
                                       th * 1024 + t2 * 512 + 512],
                                   start=(kc == 0), stop=(kc == 15)),
                          wait=wmh,
                          inc=[("t", 1)] if (kv == 3 and t2 == 1) else ())
            tk_ = C("t")
            for kv in range(4):
                bank_ev[kv * 2] = bank_ev[kv * 2 + 1] = E(
                    "scalar", lambda e, kv=kv, th=th: e.copy(
                        khat[:, kv, th * 1024:th * 1024 + 1024], bank(kv * 2, 2)),
                    wait=[("t", tk_)] if kv == 0 else (),
                    inc=[("s", 1)])
            qev = C("s")
        wv_lm = DMA("sync", lambda e: e.dma_start(
            out=wv_sb[:], in_=wvb_h.rearrange("(c p) n -> p c n", p=128)),
            wait=[("t", q_all_t)])
        rm = 0
        prev_rm = [("s", qev)]
        reg_guard = {0: 0, 4: 0}
        for wi in range(12):
            reg = (wi % 2) * 4
            if wi < 8:
                outs_ = [qhat[:, wi * 2 + b_ // 2, (b_ % 2) * 512:(b_ % 2) * 512 + 512]
                         for b_ in range(4)]
                raw = ar3[:, (wi * 2) * 1024:(wi * 2 + 2) * 1024]
                rm = rms_apply(raw, outs_, 4, reg, qg_sb, extra=prev_rm,
                               bank_guard=reg_guard[reg])
            else:
                kv = wi - 8
                outs_ = [khat[:, kv, b_ * 512:b_ * 512 + 512] for b_ in range(4)]
                raw = ar4[:, kv * 2048:(kv + 1) * 2048]
                rm = rms_apply(raw, outs_, 4, reg, kg_sb, extra=prev_rm,
                               bank_guard=reg_guard[reg])
            reg_guard[reg] = rm
            prev_rm = []
        kv_done = rm
        v_ev = [0, 0]
        for tt in range(16):
            reg = tt % 2
            wmm = [wv_lm, ("v", kv_done)] if tt < 2 else []
            if v_ev[reg]:
                wmm.append(("s", v_ev[reg]))
            for kc in range(16):
                E("tensor", lambda e, tt=tt, kc=kc, reg=reg: e.matmul(
                    bank(reg), xfT[:, kc, tt * 128:tt * 128 + 128], wv_sb[:, kc, :],
                    start=(kc == 0), stop=(kc == 15)),
                  wait=wmm if kc == 0 else (),
                  inc=[("t", 1)] if kc == 15 else ())
            vt = C("t")
            wve = [("t", vt)]
            if tt == 0:
                wve += dq_store_last
            v_ev[reg] = E("scalar", lambda e, tt=tt, reg=reg: e.copy(
                v_sb[:, tt, :], bank(reg)), wait=wve, inc=[("s", 1)])
        v_done = ("s", C("s"))
        ag_w = ag_w + [v_done]
        for sh_, fu_ in ((g_sh, g_fu), (u_sh, u_fu), (d_sh, d_fu)):
            E("gpsimd", lambda e, sh_=sh_, fu_=fu_: e.collective_compute(
                "AllGather", Alu.bypass, replica_groups=[list(range(NCORES))],
                ins=[sh_.ap().opt()], outs=[fu_.ap().opt()]),
              wait=ag_w, inc=[("cc", 1)])
            ag_w = ()
        ag_done = C("cc")


        for kc in range(16):
            DMA("sync", lambda e, kc=kc: e.dma_start(
                out=wo_sb[:, kc, :], in_=wob_h[kc * 128:kc * 128 + 128, :]),
                wait=[v_done] if kc == 0 else [])
        wo_lms = bld.dma_totals("sync")

        groups = [(qb, h) for qb in range(8) for h in range(H)]
        ng = len(groups)

        def geo(qb):
            M = MPAT[qb]
            nkt = M + 1
            ngr = (nkt + 3) // 4
            return nkt, ngr, ngr * 512

        def region(qb, h):
            ngr = geo(qb)[1]
            if ngr <= 2:
                return (h % 2) * 2
            if ngr == 3:
                return (h % 2) * 3
            return 0

        vsc_prev = 0
        bank_guard = [0, 0, 0, 0, 0, 0]
        pt_t_by_par = [0, 0]
        pt_ev = 0
        pv_ev = 0
        pvap = ps[:, 3584:3712]
        Pbuf = [P_sb, sq2]

        def emit_S(i):
            qb, h = groups[i]
            kv = h // 4
            nkt, ngr, L5 = geo(qb)
            r0 = region(qb, h)
            w = []
            gmax = max(bank_guard[r0:r0 + ngr])
            if gmax:
                w.append(("s", gmax))
            if i == 0:
                w = [v_done] + const_ms + w
            for g in range(ngr):
                E("tensor", lambda e, qb=qb, h=h, kv=kv, g=g, r0=r0: e.matmul(
                    bank(r0 + g), qhat[:, h, qb * 128:qb * 128 + 128],
                    khat[:, kv, g * 512:g * 512 + 512], start=True, stop=True),
                  wait=w if g == 0 else (),
                  inc=[("t", 1)] if g == ngr - 1 else ())
            return C("t")

        tS_ms = emit_S(0)
        for i in range(ng):
            qb, h = groups[i]
            kv = h // 4
            nkt, ngr, L5 = geo(qb)
            r0 = region(qb, h)
            Sap = ps[:, r0 * 512:r0 * 512 + L5]
            Pv = Pbuf[i % 2]
            E("vector", lambda e, qb=qb, Sap=Sap, L5=L5: e.tensor_tensor(
                Sap[:, L5 - 512:L5], Sap[:, L5 - 512:L5], masks[:, qb, :], Alu.add),
              wait=[("t", tS_ms)])
            E("vector", lambda e: e.drain())
            E("vector", lambda e, Sap=Sap: e.tensor_reduce(
                nmax[:], Sap, axis=AX, op=Alu.max, negate=True))
            E("vector", lambda e: e.drain())
            E("vector", lambda e: e.tensor_single_scalar(
                nmax[:], nmax[:], 0.08838834764831845, Alu.mult))
            vmx = vbar()
            wexp = [("v", vmx)]
            if pt_t_by_par[i % 2]:
                wexp.append(("t", pt_t_by_par[i % 2]))
            E("scalar", lambda e, Sap=Sap, L5=L5, Pv=Pv: e.activation(
                Pv[:, 0:L5], Sap, Act.Exp, bias=nmax[:],
                scale=0.08838834764831845, accum_out=rsum[:]), wait=wexp)
            em = sbar()
            for b_ in range(r0, r0 + ngr):
                bank_guard[b_] = em
            wln_ = [("v", vsc_prev)] if vsc_prev else []
            E("scalar", lambda e: e.activation(rcp[:], rsum[:], Act.Ln), wait=wln_)
            E("scalar", lambda e: e.drain())
            E("scalar", lambda e: e.activation(rcp[:], rcp[:], Act.Exp, scale=-1.0))
            smn = sbar()
            E("vector", lambda e, L5=L5, Pv=Pv: e.tensor_scalar(
                Pv[:, 0:L5], Pv[:, 0:L5], rcp[:], None, op0=Alu.mult),
              wait=[("s", smn)])
            vsc_prev = vbar()
            vsc = ("v", vsc_prev)
            if i + 1 < ng:
                tS_ms = emit_S(i + 1)
            wPT = [vsc]
            nw = (nkt + 7) // 8
            for w_ in range(nw):
                t0 = w_ * 8
                t1 = min(nkt, t0 + 8)
                for kt in range(t0, t1):
                    E("tensor", lambda e, kt=kt, t0=t0, Pv=Pv: e.transpose(
                        bank_bf(6)[:, (kt - t0) * 128:(kt - t0) * 128 + 128],
                        Pv[:, kt * 128:kt * 128 + 128], id_sb[:]),
                      wait=wPT if kt == t0 else (),
                      inc=[("t", 1)] if kt == t1 - 1 else ())
                tw = C("t")
                if w_ == nw - 1:
                    pt_t_by_par[i % 2] = tw
                pt_ev = E("scalar", lambda e, t0=t0, t1=t1: e.copy(
                    PT_sb[:, t0:t1, :],
                    bank_bf(6)[:, 0:(t1 - t0) * 128].rearrange(
                        "p (a b) -> p a b", b=128)),
                    wait=[("t", tw)], inc=[("s", 1)])
                wPT = [("s", pt_ev)]
            wPV = [("s", pt_ev)]
            if pv_ev:
                wPV.append(("s", pv_ev))
            for kt in range(nkt):
                E("tensor", lambda e, kt=kt, kv=kv: e.matmul(
                    pvap, v_sb[:, kt, kv * 128:kv * 128 + 128], PT_sb[:, kt, :],
                    start=(kt == 0), stop=(kt == nkt - 1)),
                  wait=wPV if kt == 0 else (),
                  inc=[("t", 1)] if kt == nkt - 1 else ())
            tpv = C("t")
            pv_ev = E("scalar", lambda e, h=h, qb=qb: e.copy(
                ao_sb[:, h, qb * 128:qb * 128 + 128], pvap),
                wait=[("t", tpv)], inc=[("s", 1)])
        a3_done_s = C("s")

        vst_prev = 0
        sst_prev = 0
        y1t_last_t = 0
        y1t_ev = [0, 0]
        xres_used = [0, 0]
        pend = None

        def y1t_emit(ptc, lnv):
            nonlocal_state = None
            for bh in range(2):
                wt2 = [("v", lnv)]
                if y1t_ev[bh]:
                    wt2.append(("v", y1t_ev[bh]))
                for kcl in range(8):
                    kc = bh * 8 + kcl
                    E("tensor", lambda e, bh=bh, kcl=kcl, kc=kc: e.transpose(
                        bank_bf(4 + bh)[0:64, kcl * 128:kcl * 128 + 128],
                        P_sb[:, kc * 128:kc * 128 + 128:2], id_sb[:]),
                      wait=wt2 if kcl == 0 else ())
                    E("tensor", lambda e, bh=bh, kcl=kcl, kc=kc: e.transpose(
                        bank_bf(4 + bh)[64:128, kcl * 128:kcl * 128 + 128],
                        P_sb[:, kc * 128 + 1:kc * 128 + 128:2], id_sb[:]),
                      inc=[("t", 1)] if kcl == 7 else ())
                tm2 = C("t")
                y1t_ev[bh] = E("vector", lambda e, bh=bh, ptc=ptc: e.tensor_copy(
                    y1T[:, bh * 8:bh * 8 + 8, ptc * 128:ptc * 128 + 128],
                    bank_bf(4 + bh)[:].rearrange("p (a b) -> p a b", b=128)),
                    wait=[("t", tm2)], inc=[("v", 1)])
            return C("t")

        for tc in range(8):
            bu = tc % 2
            wxr = [("s", a3_done_s)] if tc < 2 else [("v", xres_used[bu])]
            xm = DMA("sync", lambda e, tc=tc, bu=bu: e.dma_start(
                out=xres[bu][:], in_=x_q[tc * 128:tc * 128 + 128, :]),
                wait=wxr)
            wmm = [("s", a3_done_s)] + (wo_lms if tc == 0 else [])
            if vst_prev:
                wmm.append(("v", vst_prev))
            for og in range(4):
                for kc in range(16):
                    E("tensor", lambda e, tc=tc, og=og, kc=kc: e.matmul(
                        bank(og), ao_sb[:, kc, tc * 128:tc * 128 + 128],
                        wo_sb[:, kc, og * 512:og * 512 + 512],
                        start=(kc == 0), stop=(kc == 15)),
                      wait=wmm if (og == 0 and kc == 0) else (),
                      inc=[("t", 1)] if (og == 3 and kc == 15) else ())
            tm = C("t")
            if pend is not None:
                y1t_last_t = y1t_emit(*pend)
            for og in range(4):
                wstt = [("t", tm), xm] if og == 0 else []
                if og == 0 and sst_prev:
                    wstt.append(("s", sst_prev))
                E("vector", lambda e, og=og, bu=bu: e.scalar_tensor_tensor(
                    y1pre[:, og * 512:og * 512 + 512],
                    xres[bu][:, og * 512:og * 512 + 512], 2.0, bank(og),
                    op0=Alu.mult, op1=Alu.add),
                  wait=wstt)
            E("vector", lambda e: e.drain())
            E("vector", lambda e: e.tensor_reduce(nmean[:], y1pre[:], axis=AX,
                                                  op=Alu.add, negate=True))
            E("vector", lambda e: e.drain())
            E("vector", lambda e: e.tensor_single_scalar(
                nmean[:], nmean[:], 1.0 / 2048, Alu.mult))
            vst = vbar()
            xres_used[bu] = vst
            vst_prev = vst
            E("scalar", lambda e: e.activation(sq2[:, 0:2048], y1pre[:], Act.Square,
                                               bias=nmean[:], accum_out=ssd[:]),
              wait=[("v", vst)])
            E("scalar", lambda e: e.drain())
            E("scalar", lambda e: e.activation(rstd[:], ssd[:], Act.Ln,
                                               scale=1.0 / 2048, bias=eps5[:]))
            E("scalar", lambda e: e.drain())
            E("scalar", lambda e: e.activation(rstd[:], rstd[:], Act.Exp, scale=-0.5))
            sst = sbar()
            sst_prev = sst
            wln = [("s", sst)]
            if y1t_last_t:
                wln.append(("t", y1t_last_t))
            E("vector", lambda e: e.tensor_scalar(
                P_sb[:], y1pre[:], nmean[:], rstd[:], op0=Alu.add, op1=Alu.mult),
              wait=wln)
            lnv = vbar()
            pend = (tc, lnv)
        y1t_last_t = y1t_emit(*pend)
        a4_done_v = C("v")

        gu_mm = [0, 0]
        inter_ev = [0, 0]
        dn_mm = [0, 0]
        y2_add = [0, 0]
        dn_last_t = 0
        first = True
        for sc in range(8):
            for nc2 in range(16):
                nch = sc * 16 + nc2
                par = nc2 % 2
                wld = ([("t", gu_mm[par])] if gu_mm[par]
                       else [("cc", 2), ("v", a4_done_v)])
                glm1 = DMA("sync", lambda e, nch=nch, par=par: e.dma_start(
                    out=gub[:, par * 2, :, :], in_=g_fu.ap()[nch]), wait=wld)
                glm2 = DMA("sync", lambda e, nch=nch, par=par: e.dma_start(
                    out=gub[:, par * 2 + 1, :, :], in_=u_fu.ap()[nch]))
                reg = par * 4
                wmm = [glm1, glm2]
                if inter_ev[par]:
                    wmm.append(("v", inter_ev[par]))
                if sc > 0 and nc2 < 2:
                    wmm += [("v", y2_add[0]), ("v", y2_add[1])]
                if sc == 0 and nc2 < 2:
                    wmm.append(("v", a4_done_v))
                for kc in range(16):
                    for th in range(2):
                        E("tensor", lambda e, par=par, kc=kc, th=th, reg=reg:
                          e.matmul(bank(reg + th), gub[:, par * 2, kc, :],
                                   y1T[:, kc, th * 512:th * 512 + 512],
                                   start=(kc == 0), stop=(kc == 15)),
                          wait=wmm if (kc == 0 and th == 0) else ())
                        E("tensor", lambda e, par=par, kc=kc, th=th, reg=reg:
                          e.matmul(bank(reg + 2 + th), gub[:, par * 2 + 1, kc, :],
                                   y1T[:, kc, th * 512:th * 512 + 512],
                                   start=(kc == 0), stop=(kc == 15)),
                          inc=[("t", 1)] if (kc == 15 and th == 1) else ())
                gu_mm[par] = C("t")
                wge = [("t", gu_mm[par])]
                if inter_ev[par]:
                    wge.append(("v", inter_ev[par]))
                E("scalar", lambda e, par=par, reg=reg: e.activation(
                    sq2[:, par * 1024:par * 1024 + 1024], bank(reg, 2),
                    Act.Gelu_apprx_tanh), wait=wge)
                sg = sbar()
                wiv = [("s", sg)]
                if dn_last_t:
                    wiv.append(("t", dn_last_t))
                inter_ev[par] = E("vector", lambda e, nc2=nc2, par=par, reg=reg:
                                  e.tensor_tensor(
                                      inter[:, nc2, :], bank(reg + 2, 2),
                                      sq2[:, par * 1024:par * 1024 + 1024], Alu.mult),
                                  wait=wiv, inc=[("v", 1)])
            iv = C("v")
            for hc in range(16):
                par = hc % 2
                wld = [("t", dn_mm[par])] if dn_mm[par] else [("cc", ag_done),
                                                             ("v", a4_done_v)]
                dlm = DMA("sync", lambda e, sc=sc, hc=hc, par=par: e.dma_start(
                    out=dnb[:, par, :, :], in_=d_fu.ap()[sc * 16 + hc]),
                    wait=wld)
                reg = par * 2
                wmm = [dlm, ("v", iv)]
                if y2_add[par]:
                    wmm.append(("v", y2_add[par]))
                for nc2 in range(16):
                    for th in range(2):
                        E("tensor", lambda e, par=par, nc2=nc2, th=th, reg=reg:
                          e.matmul(bank(reg + th), dnb[:, par, nc2, :],
                                   inter[:, nc2, th * 512:th * 512 + 512],
                                   start=(nc2 == 0), stop=(nc2 == 15)),
                          wait=wmm if (nc2 == 0 and th == 0) else (),
                          inc=[("t", 1)] if (nc2 == 15 and th == 1) else ())
                dn_mm[par] = C("t")
                if first and hc < 2:
                    y2_add[par] = E("vector", lambda e, hc=hc, reg=reg: e.tensor_copy(
                        y2acc[:, hc, :], bank(reg, 2)),
                        wait=[("t", dn_mm[par])], inc=[("v", 1)])
                elif first:
                    y2_add[par] = E("vector", lambda e, hc=hc, reg=reg: e.tensor_copy(
                        y2acc[:, hc, :], bank(reg, 2)),
                        wait=[("t", dn_mm[par]), ("v", y2_add[par])], inc=[("v", 1)])
                else:
                    y2_add[par] = E("vector", lambda e, hc=hc, reg=reg:
                                    e.scalar_tensor_tensor(
                                        y2acc[:, hc, :], bank(reg, 2), 1.0,
                                        y2acc[:, hc, :], op0=Alu.mult, op1=Alu.add),
                                    wait=[("t", dn_mm[par]), ("v", y2_add[par])],
                                    inc=[("v", 1)])
            dn_last_t = C("t")
            first = False
        ffn_done_v = C("v")

        for kc in range(16):
            E("vector", lambda e, kc=kc: e.tensor_tensor(
                y2acc[:, kc, :], y2acc[:, kc, :], y1T[:, kc, :], Alu.add),
              wait=[("v", ffn_done_v)] if kc == 0 else ())
        E("vector", lambda e: e.drain())
        for kc in range(16):
            E("vector", lambda e, kc=kc: e.tensor_copy(y1T[:, kc, :], y2acc[:, kc, :]))
        vy2b = vbar()
        E("scalar", lambda e: e.activation(ar2[:], ar3[:], Act.Square),
          wait=[("v", vy2b)])
        ssq = sbar()
        st_ms = [0, 0]
        vvar_prev = 0
        for tb in range(2):
            w0 = [("s", ssq)] if tb == 0 else [("v", vvar_prev)]
            for kc in range(16):
                E("tensor", lambda e, tb=tb, kc=kc: e.matmul(
                    bank(0)[0:1, 0:512], ones_b[:],
                    y1T[:, kc, tb * 512:tb * 512 + 512],
                    start=(kc == 0), stop=(kc == 15)),
                  wait=w0 if kc == 0 else ())
            for kc in range(16):
                E("tensor", lambda e, tb=tb, kc=kc: e.matmul(
                    bank(1)[0:1, 0:512], ones_b[:],
                    inter[:, kc, tb * 512:tb * 512 + 512],
                    start=(kc == 0), stop=(kc == 15)),
                  inc=[("t", 1)] if kc == 15 else ())
            tsum = C("t")
            E("vector", lambda e: e.tensor_single_scalar(
                ln2a[:], bank(0)[0:1, 0:512], -1.0 / 2048, Alu.mult),
              wait=[("t", tsum)])
            E("vector", lambda e: e.tensor_single_scalar(
                ln2b[:], bank(1)[0:1, 0:512], 1.0 / 2048, Alu.mult))
            E("vector", lambda e: e.drain())
            E("vector", lambda e: e.tensor_tensor(ln2c[:], ln2a[:], ln2a[:], Alu.mult))
            E("vector", lambda e: e.drain())
            E("vector", lambda e: e.tensor_tensor(ln2b[:], ln2b[:], ln2c[:],
                                                  Alu.subtract))
            vvar_prev = vvar = vbar()
            E("scalar", lambda e: e.activation(ln2c[:], ln2b[:], Act.Ln,
                                               bias=eps5[0:1, :]),
              wait=[("v", vvar)])
            E("scalar", lambda e: e.drain())
            E("scalar", lambda e: e.activation(ln2c[:], ln2c[:], Act.Exp, scale=-0.5))
            srs = sbar()
            wbc = [("s", srs)]
            if st_ms[0]:
                wbc.append(("v", st_ms[0]))
            E("tensor", lambda e: e.matmul(bank(2), ones_f[:], ln2a[:],
                                           start=True, stop=True), wait=wbc)
            E("tensor", lambda e: e.matmul(bank(3), ones_f[:], ln2c[:],
                                           start=True, stop=True), inc=[("t", 1)])
            tbc = C("t")
            for kc in range(16):
                E("vector", lambda e, tb=tb, kc=kc: e.tensor_tensor(
                    y2acc[:, kc, tb * 512:tb * 512 + 512],
                    y2acc[:, kc, tb * 512:tb * 512 + 512], bank(2), Alu.add),
                  wait=[("t", tbc)] if kc == 0 else ())
            E("vector", lambda e: e.drain())
            for kc in range(16):
                E("vector", lambda e, tb=tb, kc=kc: e.tensor_tensor(
                    y2acc[:, kc, tb * 512:tb * 512 + 512],
                    y2acc[:, kc, tb * 512:tb * 512 + 512], bank(3), Alu.mult))
            st_ms[tb] = vbar()
        ev_fin = 0
        st_fin = [0, 0]
        for tc in range(8):
            bu = tc % 2
            wt = [("v", st_ms[tc // 4])]
            if ev_fin:
                wt.append(("v", ev_fin))
            for kc in range(16):
                bk = 4 + kc // 4
                E("tensor", lambda e, tc=tc, kc=kc, bk=bk: e.transpose(
                    bank(bk)[:, (kc % 4) * 128:(kc % 4) * 128 + 128],
                    y2acc[:, kc, tc * 128:tc * 128 + 128], idf_sb[:]),
                  wait=wt if kc == 0 else (),
                  inc=[("t", 1)] if kc == 15 else ())
            tfin = C("t")
            wev = [("t", tfin)]
            if st_fin[bu]:
                wev.append(st_fin[bu])
            for bk in range(4):
                E("vector", lambda e, bu=bu, bk=bk: e.tensor_copy(
                    y2tok[bu][:, bk * 512:bk * 512 + 512].rearrange(
                        "p (a j two) -> p a two j", a=4, two=2),
                    bank(4 + bk)[:].rearrange("p (a b) -> p a b", b=128).rearrange(
                        "p a (two j) -> p a two j", two=2)),
                  wait=wev if bk == 0 else ())
            ev_fin = vbar()
            st_fin[bu] = DMA("gpsimd", lambda e, tc=tc, bu=bu: e.dma_start(
                out=y_h[tc * 128:tc * 128 + 128, :], in_=y2tok[bu][:]),
                wait=[("v", ev_fin)])
        E("gpsimd", lambda e: e.nop(),
          wait=(bld.dma_totals("gpsimd") + bld.dma_totals("sync")
                + bld.dma_totals("scalar")))

        for ename in ENGS:
            entries = bld.entries[ename]

            def runner(eng, _entries=entries):
                for wait, f, incs in _entries:
                    for s_, v_ in wait:
                        eng.wait_ge(SEM[s_], v_)
                    ins = f(eng)
                    for s_, n_ in incs:
                        ins.then_inc(SEM[s_], n_)
            getattr(blk, ename)(runner)

    n_inst = sum(len(v) for v in bld.entries.values())
    print(f"[kernel] emitted {n_inst} entries "
          + str({k: len(v) for k, v in bld.entries.items()}))
    return nc


def kernel(**inputs):
    x = np.asarray(inputs["L0_attn_x"], dtype=np.float32)
    wq = np.asarray(inputs["L0_attn_Wq"], dtype=np.float32)
    wk = np.asarray(inputs["L0_attn_Wk"], dtype=np.float32)
    wv = np.asarray(inputs["L0_attn_Wv"], dtype=np.float32)
    wo = np.asarray(inputs["L0_attn_Wo"], dtype=np.float32)
    qg = np.ascontiguousarray(
        np.asarray(inputs["L0_attn_q_gamma"], dtype=np.float32).reshape(128, 1))
    kg = np.ascontiguousarray(
        np.asarray(inputs["L0_attn_k_gamma"], dtype=np.float32).reshape(128, 1))
    gp = np.asarray(inputs["L0_ffn_gate_W_packed"], dtype=np.int32)
    gs = np.asarray(inputs["L0_ffn_gate_scales"], dtype=np.float32)
    up = np.asarray(inputs["L0_ffn_up_W_packed"], dtype=np.int32)
    us = np.asarray(inputs["L0_ffn_up_scales"], dtype=np.float32)
    dp = np.asarray(inputs["L0_ffn_down_W_packed"], dtype=np.int32)
    ds = np.asarray(inputs["L0_ffn_down_scales"], dtype=np.float32)

    ident = np.eye(128, dtype=ml_dtypes.bfloat16)
    identf = np.eye(128, dtype=np.float32)

    in_maps = []
    for c in range(NCORES):
        b, par = c // 2, c % 2
        blocks = QBLOCKS[par]
        xq_rows = np.concatenate([x[b, g * 128:(g + 1) * 128, :] for g in blocks], 0)
        mk = np.zeros((8, 128, 512), dtype=np.float32)
        for qb, g in enumerate(blocks):
            L5 = ((MPAT[qb] + 1 + 3) // 4) * 512
            k0 = L5 - 512
            kk = k0 + np.arange(512)[None, :]
            qq = g * 128 + np.arange(128)[:, None]
            mk[qb] = np.where(kk <= qq, 0.0, NEG)
        in_maps.append({
            "x_full": np.ascontiguousarray(x[b]),
            "x_q": np.ascontiguousarray(xq_rows),
            "wq": wq, "wk": wk, "wv": wv, "wo": wo,
            "gp": np.ascontiguousarray(gp[c * 2048:(c + 1) * 2048].astype(np.int16)),
            "gs": np.ascontiguousarray(gs[c * 2048:(c + 1) * 2048]),
            "up": np.ascontiguousarray(up[c * 2048:(c + 1) * 2048].astype(np.int16)),
            "us": np.ascontiguousarray(us[c * 2048:(c + 1) * 2048]),
            "dp": np.ascontiguousarray(dp[:, c * 1024:(c + 1) * 1024].astype(np.int16)),
            "ds": np.ascontiguousarray(ds[:, c * 64:(c + 1) * 64]),
            "qg": qg, "kg": kg, "ident": ident, "identf": identf, "maskqb": mk,
        })

    nc = build_nc()
    res = run_bass_kernel_spmd(nc, in_maps, core_ids=list(range(NCORES)))

    out = np.empty((B, S, D), dtype=np.float32)
    for c in range(NCORES):
        b, par = c // 2, c % 2
        y = res.results[c]["y"]
        for qb, g in enumerate(QBLOCKS[par]):
            out[b, g * 128:(g + 1) * 128, :] = y[qb * 128:(qb + 1) * 128, :]
    return out
